# version 09
# baseline (speedup 1.0000x reference)
import contextlib
import numpy as np
import concourse.bass as bass
import concourse.mybir as mybir
from concourse.bass_utils import run_bass_kernel_spmd

F32 = mybir.dt.float32
BF16 = mybir.dt.bfloat16
I32 = mybir.dt.int32
AF = mybir.ActivationFunctionType
ALU = mybir.AluOpType
AX = mybir.AxisListType

NB, SEQ, D, HID = 4, 2048, 2048, 5632
T = 1024
TBS = 512
DC = D // 128
IC = 16
HC = HID // 128
NH, NKV, HD = 16, 2, 128
SCALE = 1.0 / float(np.sqrt(np.float64(HD)))
NW = HID * D
EPS_Q = 1e-8
EPS_LN = 1e-5


class Cnt:
    REG = []

    def __init__(self, es, nc, name):
        self.h = es.enter_context(nc.semaphore(name))
        self.v = 0
        self.name = name
        Cnt.REG.append(self)

    def inc(self, inst, k=1):
        inst.then_inc(self.h, k)
        self.v += k
        return self.v


def build_program():
    Cnt.REG = []
    nc = bass.Bass("TRN2", target_bir_lowering=False,
                   detect_race_conditions=False)

    XT = nc.dram_tensor("XT", [D, T], F32, kind="ExternalInput").ap()
    W1L = nc.dram_tensor("W1L", [HC, 128, 2048], F32, kind="ExternalInput").ap()
    W2L = nc.dram_tensor("W2L", [HC, 128, 2048], F32, kind="ExternalInput").ap()
    W3L = nc.dram_tensor("W3L", [HC, 128, 2048], F32, kind="ExternalInput").ap()
    WQL = nc.dram_tensor("WQL", [NH, 128, 2048], F32, kind="ExternalInput").ap()
    WKL = nc.dram_tensor("WKL", [NKV, 128, 2048], F32, kind="ExternalInput").ap()
    WVL = nc.dram_tensor("WVL", [NKV, 128, 2048], F32, kind="ExternalInput").ap()
    WOL = nc.dram_tensor("WOL", [DC, 128, 2048], F32, kind="ExternalInput").ap()
    BQ = nc.dram_tensor("BQ", [128, NH], F32, kind="ExternalInput").ap()
    BK = nc.dram_tensor("BK", [128, NKV], F32, kind="ExternalInput").ap()
    BV = nc.dram_tensor("BV", [128, NKV], F32, kind="ExternalInput").ap()
    BO = nc.dram_tensor("BO", [128, DC], F32, kind="ExternalInput").ap()
    IDENT = nc.dram_tensor("IDENT", [128, 128], F32, kind="ExternalInput").ap()
    HALF = nc.dram_tensor("HALF", [1, 1], I32, kind="ExternalInput").ap()
    NONCE = nc.dram_tensor("NONCE", [1, 1], I32, kind="ExternalInput").ap()
    OUT = nc.dram_tensor("OUT", [D, T], F32, kind="ExternalOutput").ap()

    Q12S = nc.dram_tensor("Q12S", [HC, 2, 128, 2048], BF16, kind="Internal").ap()
    Q3S = nc.dram_tensor("Q3S", [HC, 128, 2048], BF16, kind="Internal").ap()
    HIDS = nc.dram_tensor("HIDS", [HC, 2, 128, TBS], BF16, kind="Internal").ap()
    XKV = nc.dram_tensor("XKV", [2, 4, 128, T], BF16, kind="Internal",
                         addr_space="Shared").ap()
    XG = nc.dram_tensor("XG", [2, 4], F32, kind="Internal", addr_space="Shared").ap()
    XGF = nc.dram_tensor("XGF", [2, 16], I32, kind="Internal", addr_space="Shared").ap()
    XKF = nc.dram_tensor("XKF", [2, 16], I32, kind="Internal", addr_space="Shared").ap()

    es = contextlib.ExitStack()

    def sb(name, shape, dtype, st_=None):
        return (st_ or es).enter_context(nc.sbuf_tensor(name, shape, dtype))

    def psum(name, shape, dtype=F32, st_=None):
        return (st_ or es).enter_context(nc.psum_tensor(name, shape, dtype))

    hT = sb("hT", [128, DC * T], F32)
    nT = sb("nT", [128, DC * T], BF16)
    rrep = sb("rrep", [128, T], F32)
    ones_col_f = sb("ones_col_f", [128, 1], F32)
    ones_col_b = sb("ones_col_b", [128, 1], BF16)
    ones_row_f = sb("ones_row_f", [1, 128], F32)
    ones_row_b = sb("ones_row_b", [1, 128], BF16)
    negones_row_f = sb("negones_row_f", [1, 128], F32)
    bq_sb = sb("bq_sb", [128, NH], F32)
    bk_sb = sb("bk_sb", [128, NKV], F32)
    bv_sb = sb("bv_sb", [128, NKV], F32)
    bo_sb = sb("bo_sb", [128, DC], F32)
    idb = sb("idb", [128, 128], BF16)
    half_sb = sb("half_sb", [1, 1], I32)
    nonce_sb = sb("nonce_sb", [1, 16], I32)
    flag_sb = sb("flag_sb", [1, 16], I32)
    thr_sb = sb("thr_sb", [128, 8], F32)
    rows_sb = sb("rows_sb", [1, 64], F32)
    g3row = sb("g3row", [1, 4], F32)
    growB = sb("growB", [1, 4], F32)
    kmax2_sb = sb("kmax2_sb", [1, 2], F32)
    bump_sb = sb("bump_sb", [1, 8], F32)

    rh = es.enter_context(nc.gpsimd.register("rh"))
    rn = es.enter_context(nc.gpsimd.register("rn"))
    rf = es.enter_context(nc.gpsimd.register("rf"))
    rp = es.enter_context(nc.gpsimd.register("rp"))
    rit = es.enter_context(nc.gpsimd.register("rit"))
    qrh = es.enter_context(nc.sync.register("qrh"))

    sv = Cnt(es, nc, "sv")
    svh = Cnt(es, nc, "svh")
    ss = Cnt(es, nc, "ss")
    st = Cnt(es, nc, "st")
    sg = Cnt(es, nc, "sg")
    sd_x = Cnt(es, nc, "sd_x")
    sd_b = Cnt(es, nc, "sd_b")
    sd_out = Cnt(es, nc, "sd_out")
    sdG = [Cnt(es, nc, f"sdG{i}") for i in range(4)]
    sdA = [Cnt(es, nc, f"sdA{i}") for i in range(2)]
    sdB = [Cnt(es, nc, f"sdB{i}") for i in range(4)]
    sdS = [Cnt(es, nc, f"sdS{i}") for i in range(2)]
    sdT = [Cnt(es, nc, f"sdT{i}") for i in range(2)]
    sdH = [Cnt(es, nc, f"sdH{i}") for i in range(4)]
    sdR = [Cnt(es, nc, f"sdR{i}") for i in range(4)]
    sgq = [Cnt(es, nc, f"sgq{i}") for i in range(2)]
    sgkv = Cnt(es, nc, "sgkv")
    sgfl = Cnt(es, nc, "sgfl")
    sgpl = Cnt(es, nc, "sgpl")
    sgin = Cnt(es, nc, "sgin")
    sgb = Cnt(es, nc, "sgb")

    thr_marks = {}

    def hcol(c, t0=0, t1=T):
        return slice(c * T + t0, c * T + t1)

    with nc.Block() as block:

        def barrier():
            snap = [(c, c.v) for c in Cnt.REG if c.v > 0]

            def waits(q, own):
                for c, v in snap:
                    if c is not own:
                        q.wait_ge(c.h, v)

            @block.sync
            def _(q):
                waits(q, None)

            @block.vector
            def _(v):
                waits(v, sv)

            @block.scalar
            def _(s):
                waits(s, ss)

            @block.tensor
            def _(t):
                waits(t, st)

            @block.gpsimd
            def _(g):
                waits(g, sg)

        @block.sync
        def _(q):
            for c in range(DC):
                q.dma_start(out=hT[:, hcol(c)], in_=XT[c * 128:(c + 1) * 128, :]
                            ).then_inc(sd_x.h, 16)
            sd_x.v += 16 * DC
            for dst, src in ((bq_sb, BQ), (bk_sb, BK), (bv_sb, BV), (bo_sb, BO)):
                q.dma_start(out=dst[:], in_=src).then_inc(sd_b.h, 16)
            q.dma_start(out=half_sb[:], in_=HALF).then_inc(sd_b.h, 16)
            q.dma_start(out=nonce_sb[:, 0:1], in_=NONCE).then_inc(sd_b.h, 16)
            sd_b.v += 16 * 6

        @block.gpsimd
        def _(g):
            g.dma_start(out=idb[:], in_=IDENT).then_inc(sgb.h, 16)
            sgb.v += 16

        @block.vector
        def _(v):
            v.memset(ones_col_f[:], 1.0)
            v.memset(ones_col_b[:], 1.0)
            v.memset(ones_row_f[:], 1.0)
            v.memset(ones_row_b[:], 1.0)
            v.memset(negones_row_f[:], -1.0)
            v.memset(g3row[:], 0.0)
            v.memset(growB[:], 0.0)
            v.memset(nonce_sb[:, 1:16], 0)
            sv.inc(v.memset(bump_sb[:], 0.0))

        esg = contextlib.ExitStack()
        gsl = [sb(f"gsl{i}", [128, 2048], F32, esg) for i in range(4)]
        gpart = sb("gpart", [128, 144], F32, esg)
        gdump = sb("gdump", [128, 2048], F32, esg)
        ps_g3 = psum("ps_g3", [1, 4], F32, esg)
        ps_thr = psum("ps_thr", [128, 8], F32, esg)
        NG = 3 * HC

        sv_g_base = sv.v
        ss_g_base = ss.v

        def g_red_mark(i):
            if i % 2 == 0:
                return sv_g_base + i // 2 + 1 + (6 if i >= 2 * HC else 0)
            return ss_g_base + i // 2 + 1

        @block.gpsimd
        def _(g):
            for i in range(NG):
                slot = i % 4
                mat, j = i // HC, i % HC
                W = (W1L, W2L, W3L)[mat]
                if i >= 4:
                    ip = i - 4
                    if ip % 2 == 0:
                        g.wait_ge(sv.h, g_red_mark(ip))
                    else:
                        g.wait_ge(ss.h, g_red_mark(ip))
                g.dma_start(out=gsl[slot][:], in_=W[j, :, :]
                            ).then_inc(sdG[slot].h, 16)
                sdG[slot].v += 16

        @block.vector
        def _(v):
            for i in range(0, 2 * HC, 2):
                v.wait_ge(sdG[i % 4].h, 16 * (i // 4 + 1))
                sv.inc(v.reduce_sum(gpart[:, i:i + 1], gsl[i % 4][:], axis=AX.X,
                                    apply_absolute_value=True))
                assert sv.v == g_red_mark(i)
            v.wait_ge(ss.h, g_red_mark(2 * HC - 1))
            for m in range(2):
                v.wait_ge(sv.h, sv.v)
                sv.inc(v.reduce_sum(gpart[:, 136 + m:137 + m],
                                    gpart[:, m * HC:(m + 1) * HC], axis=AX.X))
        sv_gpart12 = sv.v

        @block.scalar
        def _(s):
            for i in range(1, 2 * HC, 2):
                s.wait_ge(sdG[i % 4].h, 16 * (i // 4 + 1))
                ss.inc(s.activation(gdump[:], gsl[i % 4][:], AF.Abs,
                                    accum_out=gpart[:, i:i + 1]))
                assert ss.v == g_red_mark(i)

        @block.tensor
        def _(t):
            t.wait_ge(sv.h, sv_gpart12)
            for m in range(2):
                mm = t.matmul(ps_g3[:, m:m + 1], gpart[:, 136 + m:137 + m],
                              ones_col_f[:], start=True, stop=True)
            st.inc(mm)
        st_g12 = st.v

        @block.vector
        def _(v):
            v.wait_ge(st.h, st_g12)
            sv.inc(v.tensor_copy(g3row[:, 0:2], ps_g3[:, 0:2]))
            v.wait_ge(sv.h, sv.v)
            v.tensor_scalar(rows_sb[:, 8:10], g3row[:, 0:2],
                            0.5 / NW, 0.5 * EPS_Q, op0=ALU.mult, op1=ALU.add)
            sv.inc(v.tensor_scalar(rows_sb[:, 11:13], g3row[:, 0:2],
                                   -0.5 / NW, -0.5 * EPS_Q, op0=ALU.mult, op1=ALU.add))
        sv_throw12 = sv.v

        @block.tensor
        def _(t):
            t.wait_ge(sv.h, sv_throw12)
            st.inc(t.matmul(ps_thr[:, 0:2], ones_row_f[:], rows_sb[:, 8:10],
                            start=True, stop=True))
            st.inc(t.matmul(ps_thr[:, 3:5], ones_row_f[:], rows_sb[:, 11:13],
                            start=True, stop=True))
        st_thr12 = st.v

        @block.vector
        def _(v):
            v.wait_ge(st.h, st_thr12)
            sv.inc(v.tensor_copy(thr_sb[:, 0:2], ps_thr[:, 0:2]))
            sv.inc(v.tensor_copy(thr_sb[:, 3:5], ps_thr[:, 3:5]))
        sv_thr12 = sv.v

        @block.vector
        def _(v):
            for i in range(2 * HC, NG, 2):
                v.wait_ge(sdG[i % 4].h, 16 * (i // 4 + 1))
                sv.inc(v.reduce_sum(gpart[:, i:i + 1], gsl[i % 4][:], axis=AX.X,
                                    apply_absolute_value=True))
                assert sv.v == g_red_mark(i)
            v.wait_ge(ss.h, g_red_mark(NG - 1))
            v.wait_ge(sv.h, sv.v)
            sv.inc(v.reduce_sum(gpart[:, 138:139], gpart[:, 2 * HC:NG], axis=AX.X))
        sv_gpart3 = sv.v

        @block.scalar
        def _(s):
            for i in range(2 * HC + 1, NG, 2):
                s.wait_ge(sdG[i % 4].h, 16 * (i // 4 + 1))
                ss.inc(s.activation(gdump[:], gsl[i % 4][:], AF.Abs,
                                    accum_out=gpart[:, i:i + 1]))
                assert ss.v == g_red_mark(i)

        @block.tensor
        def _(t):
            t.wait_ge(sv.h, sv_gpart3)
            st.inc(t.matmul(ps_g3[:, 2:3], gpart[:, 138:139],
                            ones_col_f[:], start=True, stop=True))
        st_g3 = st.v

        @block.vector
        def _(v):
            v.wait_ge(st.h, st_g3)
            sv.inc(v.tensor_copy(g3row[:, 2:3], ps_g3[:, 2:3]))
            v.wait_ge(sv.h, sv.v)
            v.tensor_scalar(rows_sb[:, 10:11], g3row[:, 2:3],
                            0.5 / NW, 0.5 * EPS_Q, op0=ALU.mult, op1=ALU.add)
            sv.inc(v.tensor_scalar(rows_sb[:, 13:14], g3row[:, 2:3],
                                   -0.5 / NW, -0.5 * EPS_Q, op0=ALU.mult, op1=ALU.add))
        sv_throw3 = sv.v

        @block.tensor
        def _(t):
            t.wait_ge(sv.h, sv_throw3)
            st.inc(t.matmul(ps_thr[:, 2:3], ones_row_f[:], rows_sb[:, 10:11],
                            start=True, stop=True))
            st.inc(t.matmul(ps_thr[:, 5:6], ones_row_f[:], rows_sb[:, 13:14],
                            start=True, stop=True))
        st_thr3 = st.v

        @block.vector
        def _(v):
            v.wait_ge(st.h, st_thr3)
            sv.inc(v.tensor_copy(thr_sb[:, 2:3], ps_thr[:, 2:3]))
            sv.inc(v.tensor_copy(thr_sb[:, 5:6], ps_thr[:, 5:6]))
        sv_thr3 = sv.v
        thr_marks["thr12"] = sv_thr12
        thr_marks["thr3"] = sv_thr3

        def emit_ln(tag):
            esl = contextlib.ExitStack()
            sqr = [sb(f"sqr{tag}{i}", [128, T], BF16, esl) for i in range(2)]
            ln_tmp = sb(f"ln_tmp{tag}", [128, T], F32, esl)
            brep = sb(f"brep{tag}", [128, T], F32, esl)
            lnrow = sb(f"lnrow{tag}", [1, 6 * TBS], F32, esl)
            ps_sum = [psum(f"ps_sum{tag}{i}", [1, TBS], F32, esl) for i in range(2)]
            ps_sq = [psum(f"ps_sq{tag}{i}", [1, TBS], F32, esl) for i in range(2)]
            ps_bc = psum(f"ps_bc{tag}", [128, T], F32, esl)

            sv_base, st_base = sv.v, st.v

            def cast_mark(c):
                return sv_base + c + 1

            def stats_mark(c):
                return st_base + 4 * (c + 1)

            @block.vector
            def _(v):
                for c in range(DC):
                    if c >= 2:
                        v.wait_ge(st.h, stats_mark(c - 2))
                    v.tensor_copy(nT[:, hcol(c)], hT[:, hcol(c)])
                    v.wait_ge(sv.h, sv.v)
                    sv.inc(v.tensor_mul(sqr[c % 2][:], nT[:, hcol(c)], nT[:, hcol(c)]))
                    assert sv.v == cast_mark(c)

            @block.tensor
            def _(t):
                for c in range(DC):
                    t.wait_ge(sv.h, cast_mark(c))
                    for tb in range(2):
                        st.inc(t.matmul(ps_sum[tb][:], ones_col_b[:],
                                        nT[:, hcol(c, tb * TBS, (tb + 1) * TBS)],
                                        start=(c == 0), stop=(c == DC - 1)))
                        st.inc(t.matmul(ps_sq[tb][:], ones_col_b[:],
                                        sqr[c % 2][:, tb * TBS:(tb + 1) * TBS],
                                        start=(c == 0), stop=(c == DC - 1)))
                    assert st.v == stats_mark(c)
            st_stats = st.v

            @block.vector
            def _(v):
                v.wait_ge(st.h, st_stats)
                for tb in range(2):
                    o = 3 * tb * TBS
                    sv.inc(v.tensor_single_scalar(lnrow[:, o:o + TBS], ps_sum[tb][:],
                                                  1.0 / D, op=ALU.mult))
                    v.wait_ge(sv.h, sv.v)
                    sv.inc(v.tensor_mul(lnrow[:, o + TBS:o + 2 * TBS],
                                        lnrow[:, o:o + TBS], lnrow[:, o:o + TBS]))
                    sv.inc(v.tensor_scalar(lnrow[:, o + 2 * TBS:o + 3 * TBS],
                                           ps_sq[tb][:], 1.0 / D, EPS_LN,
                                           op0=ALU.mult, op1=ALU.add))
                    v.wait_ge(sv.h, sv.v)
                    sv.inc(v.tensor_sub(lnrow[:, o + 2 * TBS:o + 3 * TBS],
                                        lnrow[:, o + 2 * TBS:o + 3 * TBS],
                                        lnrow[:, o + TBS:o + 2 * TBS]))
            sv_var = sv.v

            @block.scalar
            def _(s):
                s.wait_ge(sv.h, sv_var)
                for tb in range(2):
                    o = 3 * tb * TBS
                    ss.inc(s.activation(lnrow[:, o + 2 * TBS:o + 3 * TBS],
                                        lnrow[:, o + 2 * TBS:o + 3 * TBS], AF.Ln))
                    s.wait_ge(ss.h, ss.v)
                    ss.inc(s.activation(lnrow[:, o + 2 * TBS:o + 3 * TBS],
                                        lnrow[:, o + 2 * TBS:o + 3 * TBS],
                                        AF.Exp, scale=-0.5))
            ss_r = ss.v

            @block.vector
            def _(v):
                v.wait_ge(ss.h, ss_r)
                for tb in range(2):
                    o = 3 * tb * TBS
                    sv.inc(v.scalar_tensor_tensor(
                        lnrow[:, o:o + TBS], lnrow[:, o:o + TBS], -1.0,
                        lnrow[:, o + 2 * TBS:o + 3 * TBS],
                        op0=ALU.mult, op1=ALU.mult))
            sv_rows = sv.v

            @block.tensor
            def _(t):
                t.wait_ge(sv.h, sv_rows)
                for tb in range(2):
                    o = 3 * tb * TBS
                    st.inc(t.matmul(ps_bc[:, tb * TBS:(tb + 1) * TBS], ones_row_f[:],
                                    lnrow[:, o + 2 * TBS:o + 3 * TBS],
                                    start=True, stop=True))
            st_bc1 = st.v

            @block.vector
            def _(v):
                v.wait_ge(st.h, st_bc1)
                sv.inc(v.tensor_copy(rrep[:], ps_bc[:]))
            sv_rrep = sv.v

            @block.tensor
            def _(t):
                t.wait_ge(sv.h, sv_rrep)
                for tb in range(2):
                    o = 3 * tb * TBS
                    st.inc(t.matmul(ps_bc[:, tb * TBS:(tb + 1) * TBS], ones_row_f[:],
                                    lnrow[:, o:o + TBS], start=True, stop=True))
            st_bc2 = st.v

            @block.vector
            def _(v):
                v.wait_ge(st.h, st_bc2)
                sv.inc(v.tensor_copy(brep[:], ps_bc[:]))
                v.wait_ge(sv.h, sv.v)
                for c in range(DC):
                    sv.inc(v.tensor_mul(ln_tmp[:], hT[:, hcol(c)], rrep[:]))
                    v.wait_ge(sv.h, sv.v)
                    sv.inc(v.tensor_add(nT[:, hcol(c)], ln_tmp[:], brep[:]))
                    v.wait_ge(sv.h, sv.v)

            barrier()
            esl.close()

        def emit_ffn(app):
            esl = contextlib.ExitStack()
            w12raw = [sb(f"w12r{app}{i}", [128, 2048], F32, esl) for i in range(2)]
            q12 = [sb(f"q12_{app}{i}", [128, 4096], BF16, esl) for i in range(2)]
            maskr = [sb(f"mask{app}{i}", [128, 1024], BF16, esl) for i in range(4)]
            if app == 1:
                w3A = [sb(f"w3A{i}", [128, 2048], F32, esl) for i in range(2)]
                q3A = [sb(f"q3A{i}", [128, 2048], BF16, esl) for i in range(2)]
                mask3 = [sb(f"mask3{i}", [128, 2048], BF16, esl) for i in range(2)]
            sig = [sb(f"sig{app}{i}", [128, TBS], BF16, esl) for i in range(4)]
            hid_ring = [sb(f"hidr{app}{i}", [128, TBS], BF16, esl) for i in range(4)]

            esa = contextlib.ExitStack()
            ps_x1 = [psum(f"psx1{app}{i}", [128, TBS], F32, esa) for i in range(4)]
            ps_x2 = [psum(f"psx2{app}{i}", [128, TBS], F32, esa) for i in range(4)]

            NU = 2 * HC
            sdA_b = [sdA[i].v for i in range(2)]
            sdS_b = [sdS[i].v for i in range(2)]
            sdH_b = [sdH[i].v for i in range(4)]
            sg_b, svq_b, ss_b, st_b = sg.v, sv.v, ss.v, st.v
            svh_b = svh.v
            sdB_bA = [sdB[i].v for i in range(2)]
            sdT_bA = [sdT[i].v for i in range(2)]

            def a_load_mark(u):
                return sdA_b[u % 2] + 32 * (u // 2 + 1)

            def a2_load_mark(c):
                return sdA_b[c % 2] + 32 * (c // 2 + 1)

            def a_quant_mark(u):
                c, hh = u // 2, u % 2
                return svq_b + 10 * c + 4 * (hh + 1)

            def a_w3q_mark(c):
                return svq_b + 10 * (c + 1)

            def a_w3load_mark(c):
                return sdB_bA[c % 2] + 16 * (c // 2 + 1)

            def a_w3scr_mark(c):
                return sdT_bA[c % 2] + 16 * (c // 2 + 1)

            def a_scr_mark(u):
                return sdS_b[u % 2] + 32 * (u // 2 + 1)

            def a_mm_mark(c):
                return st_b + 4 * (c + 1)

            def a_sig_mark(c, tb):
                return ss_b + 2 * c + tb + 1

            def a_hid_mark(c, tb):
                return svh_b + 2 * c + tb + 1

            def a_spill_mark(c, tb):
                k = 2 * c + tb
                return sdH_b[k % 4] + 16 * (k // 4 + 1)

            @block.sync
            def _(q):
                if app == 1:
                    for u in range(NU):
                        slot, c, hh = u % 2, u // 2, u % 2
                        if u >= 2:
                            q.wait_ge(sv.h, a_quant_mark(u - 2))
                        q.dma_start(out=w12raw[slot][:, 0:1024],
                                    in_=W1L[c, :, hh * 1024:(hh + 1) * 1024]
                                    ).then_inc(sdA[slot].h, 16)
                        q.dma_start(out=w12raw[slot][:, 1024:2048],
                                    in_=W2L[c, :, hh * 1024:(hh + 1) * 1024]
                                    ).then_inc(sdA[slot].h, 16)
                        sdA[slot].v += 32
                        assert sdA[slot].v == a_load_mark(u)
                        if hh == 0:
                            if c >= 2:
                                q.wait_ge(sv.h, a_w3q_mark(c - 2))
                            q.dma_start(out=w3A[c % 2][:], in_=W3L[c, :, :]
                                        ).then_inc(sdB[c % 2].h, 16)
                            sdB[c % 2].v += 16
                            assert sdB[c % 2].v == a_w3load_mark(c)
                        else:
                            if c >= 2:
                                cq = c - 2
                                q.wait_ge(sv.h, a_w3q_mark(cq))
                                q.dma_start(out=Q3S[cq, :, :], in_=q3A[cq % 2][:]
                                            ).then_inc(sdT[cq % 2].h, 16)
                                sdT[cq % 2].v += 16
                                assert sdT[cq % 2].v == a_w3scr_mark(cq)
                        if u >= 2:
                            up = u - 2
                            ps_, cp, hp = up % 2, up // 2, up % 2
                            q.wait_ge(sv.h, a_quant_mark(up))
                            q.dma_start(out=Q12S[cp, 0, :, hp * 1024:(hp + 1) * 1024],
                                        in_=q12[cp % 2][:, hp * 1024:(hp + 1) * 1024]
                                        ).then_inc(sdS[ps_].h, 16)
                            q.dma_start(out=Q12S[cp, 1, :, hp * 1024:(hp + 1) * 1024],
                                        in_=q12[cp % 2][:, 2048 + hp * 1024:
                                                       2048 + (hp + 1) * 1024]
                                        ).then_inc(sdS[ps_].h, 16)
                            sdS[ps_].v += 32
                            assert sdS[ps_].v == a_scr_mark(up)
                        ku = u - 8
                        if ku >= 0:
                            cs, tbs_ = ku // 2, ku % 2
                            q.wait_ge(svh.h, a_hid_mark(cs, tbs_))
                            q.dma_start(out=HIDS[cs, tbs_, :, :],
                                        in_=hid_ring[ku % 4][:]).then_inc(sdH[ku % 4].h, 16)
                            sdH[ku % 4].v += 16
                            assert sdH[ku % 4].v == a_spill_mark(cs, tbs_)
                    for cq in (HC - 2, HC - 1):
                        q.wait_ge(sv.h, a_w3q_mark(cq))
                        q.dma_start(out=Q3S[cq, :, :], in_=q3A[cq % 2][:]
                                    ).then_inc(sdT[cq % 2].h, 16)
                        sdT[cq % 2].v += 16
                        assert sdT[cq % 2].v == a_w3scr_mark(cq)
                    for up in range(NU - 2, NU):
                        ps_, cp, hp = up % 2, up // 2, up % 2
                        q.wait_ge(sv.h, a_quant_mark(up))
                        q.dma_start(out=Q12S[cp, 0, :, hp * 1024:(hp + 1) * 1024],
                                    in_=q12[cp % 2][:, hp * 1024:(hp + 1) * 1024]
                                    ).then_inc(sdS[ps_].h, 16)
                        q.dma_start(out=Q12S[cp, 1, :, hp * 1024:(hp + 1) * 1024],
                                    in_=q12[cp % 2][:, 2048 + hp * 1024:
                                                   2048 + (hp + 1) * 1024]
                                    ).then_inc(sdS[ps_].h, 16)
                        sdS[ps_].v += 32
                        assert sdS[ps_].v == a_scr_mark(up)
                    for ku in range(NU - 8, NU):
                        cs, tbs_ = ku // 2, ku % 2
                        q.wait_ge(svh.h, a_hid_mark(cs, tbs_))
                        q.dma_start(out=HIDS[cs, tbs_, :, :],
                                    in_=hid_ring[ku % 4][:]).then_inc(sdH[ku % 4].h, 16)
                        sdH[ku % 4].v += 16
                        assert sdH[ku % 4].v == a_spill_mark(cs, tbs_)
                else:
                    for c in range(HC):
                        slot = c % 2
                        if c >= 2:
                            q.wait_ge(st.h, a_mm_mark(c - 2))
                        q.dma_start(out=q12[slot][:, 0:2048], in_=Q12S[c, 0, :, :]
                                    ).then_inc(sdA[slot].h, 16)
                        q.dma_start(out=q12[slot][:, 2048:4096], in_=Q12S[c, 1, :, :]
                                    ).then_inc(sdA[slot].h, 16)
                        sdA[slot].v += 32
                        assert sdA[slot].v == a2_load_mark(c)
                        ku = 2 * c - 8
                        for k in (ku, ku + 1):
                            if 0 <= k:
                                cs, tbs_ = k // 2, k % 2
                                q.wait_ge(svh.h, a_hid_mark(cs, tbs_))
                                q.dma_start(out=HIDS[cs, tbs_, :, :],
                                            in_=hid_ring[k % 4][:]
                                            ).then_inc(sdH[k % 4].h, 16)
                                sdH[k % 4].v += 16
                                assert sdH[k % 4].v == a_spill_mark(cs, tbs_)
                    for k in range(2 * HC - 8, 2 * HC):
                        cs, tbs_ = k // 2, k % 2
                        q.wait_ge(svh.h, a_hid_mark(cs, tbs_))
                        q.dma_start(out=HIDS[cs, tbs_, :, :], in_=hid_ring[k % 4][:]
                                    ).then_inc(sdH[k % 4].h, 16)
                        sdH[k % 4].v += 16
                        assert sdH[k % 4].v == a_spill_mark(cs, tbs_)



            @block.tensor
            def _(t):
                for c in range(HC):
                    slot = c % 2
                    if app == 1:
                        t.wait_ge(sv.h, a_quant_mark(2 * c + 1))
                    else:
                        t.wait_ge(sdA[slot].h, a2_load_mark(c))
                    for mat in range(2):
                        psl = (ps_x1, ps_x2)[mat]
                        for tb in range(2):
                            bank = psl[2 * tb + (c % 2)]
                            if c >= 2:
                                if mat == 0:
                                    t.wait_ge(ss.h, a_sig_mark(c - 2, tb))
                                else:
                                    t.wait_ge(svh.h, a_hid_mark(c - 2, tb))
                            for ic in range(IC):
                                mm = t.matmul(
                                    bank[:],
                                    q12[slot][:, mat * 2048 + ic * 128:
                                              mat * 2048 + ic * 128 + 128],
                                    nT[:, hcol(ic, tb * TBS, (tb + 1) * TBS)],
                                    start=(ic == 0), stop=(ic == IC - 1))
                            st.inc(mm)
                    assert st.v == a_mm_mark(c)

            @block.scalar
            def _(s):
                for c in range(HC):
                    for tb in range(2):
                        s.wait_ge(st.h, st_b + 4 * c + tb + 1)
                        if c >= 2:
                            s.wait_ge(svh.h, a_hid_mark(c - 2, tb))
                        ss.inc(s.activation(sig[2 * tb + (c % 2)][:],
                                            ps_x1[2 * tb + (c % 2)][:], AF.Sigmoid))
                        assert ss.v == a_sig_mark(c, tb)

            @block.vector
            def _(v):
                def emit_quant(u):
                    slot, c, hh = u % 2, u // 2, u % 2
                    if u == 0 and app == 1:
                        v.wait_ge(sv.h, thr_marks["thr12"])
                    v.wait_ge(sdA[slot].h, a_load_mark(u))
                    if u >= 4:
                        v.wait_ge(sdS[(u - 4) % 2].h, a_scr_mark(u - 4))
                    if c >= 2 and hh == 0:
                        v.wait_ge(st.h, a_mm_mark(c - 2))
                    for mat in range(2):
                        sv.inc(v.tensor_single_scalar(
                            maskr[(2 * u + mat) % 4][:],
                            w12raw[slot][:, mat * 1024:(mat + 1) * 1024],
                            thr_sb[:, 3 + mat:4 + mat], op=ALU.is_lt))
                    v.wait_ge(sv.h, sv.v)
                    for mat in range(2):
                        sv.inc(v.scalar_tensor_tensor(
                            q12[c % 2][:, mat * 2048 + hh * 1024:
                                       mat * 2048 + (hh + 1) * 1024],
                            w12raw[slot][:, mat * 1024:(mat + 1) * 1024],
                            thr_sb[:, mat:mat + 1], maskr[(2 * u + mat) % 4][:],
                            op0=ALU.is_gt, op1=ALU.subtract))
                    assert sv.v == a_quant_mark(u)

                def emit_hid(c, tb):
                    k = 2 * c + tb
                    v.wait_ge(ss.h, a_sig_mark(c, tb))
                    v.wait_ge(st.h, st_b + 4 * c + 2 + tb + 1)
                    if k >= 4:
                        v.wait_ge(sdH[k % 4].h,
                                  a_spill_mark((k - 4) // 2, (k - 4) % 2))
                    svh.inc(v.scalar_tensor_tensor(
                        hid_ring[k % 4][:], ps_x2[2 * tb + (c % 2)][:], 1.0,
                        sig[2 * tb + (c % 2)][:], op0=ALU.mult, op1=ALU.mult))
                    assert svh.v == a_hid_mark(c, tb)

                for c in range(HC):
                    if app == 1:
                        emit_quant(2 * c)
                        emit_quant(2 * c + 1)
                        if c == 0:
                            v.wait_ge(sv.h, thr_marks["thr3"])
                        v.wait_ge(sdB[c % 2].h, a_w3load_mark(c))
                        if c >= 4:
                            v.wait_ge(sdT[(c - 4) % 2].h, a_w3scr_mark(c - 4))
                        sv.inc(v.tensor_single_scalar(
                            mask3[c % 2][:], w3A[c % 2][:],
                            thr_sb[:, 5:6], op=ALU.is_lt))
                        v.wait_ge(sv.h, sv.v)
                        sv.inc(v.scalar_tensor_tensor(
                            q3A[c % 2][:], w3A[c % 2][:], thr_sb[:, 2:3],
                            mask3[c % 2][:], op0=ALU.is_gt, op1=ALU.subtract))
                        assert sv.v == a_w3q_mark(c)
                    if c >= 2:
                        emit_hid(c - 2, 0)
                        emit_hid(c - 2, 1)
                for c in (HC - 2, HC - 1):
                    emit_hid(c, 0)
                    emit_hid(c, 1)

            esa.close()
            q3 = [sb(f"q3_{app}{i}", [128, 1024], BF16, esl) for i in range(4)]
            hidl = [sb(f"hidl{app}{i}", [128, TBS], BF16, esl) for i in range(4)]
            esb = contextlib.ExitStack()

            sdB_b = [sdB[i].v for i in range(4)]
            sdR_b = [sdR[i].v for i in range(4)]
            sv3_b, st3_b = sv.v, st.v
            NSW = 4

            def b_load_mark(i):
                return sdB_b[i % 4] + 16 * (i // 4 + 1)

            def b_hid_mark(i):
                return sdR_b[i % 4] + 16 * (i // 4 + 1)

            def b_mm_mark(i):
                return st3_b + i + 1

            def b_evac_mark(swp):
                return sv3_b + 8 * (swp + 1)

            def b_evac_bank_mark(swp, m8):
                return sv3_b + 8 * swp + m8 + 1

            @block.sync
            def _(q):
                for swp in range(NSW):
                    tb, s_ = swp // 2, swp % 2
                    for c in range(HC):
                        i = swp * HC + c
                        slot = i % 4
                        if i >= 4:
                            q.wait_ge(st.h, b_mm_mark(i - 4))
                        if app == 1 and swp == 0:
                            q.wait_ge(sdT[c % 2].h, a_w3scr_mark(c))
                        q.dma_start(out=q3[slot][:],
                                    in_=Q3S[c, :, s_ * 1024:(s_ + 1) * 1024]
                                    ).then_inc(sdB[slot].h, 16)
                        sdB[slot].v += 16
                        assert sdB[slot].v == b_load_mark(i)

            @block.gpsimd
            def _(g):
                for swp in range(NSW):
                    tb, s_ = swp // 2, swp % 2
                    for c in range(HC):
                        i = swp * HC + c
                        if i >= 4:
                            g.wait_ge(st.h, b_mm_mark(i - 4))
                        g.wait_ge(sdH[(2 * c + tb) % 4].h, a_spill_mark(c, tb))
                        g.dma_start(out=hidl[i % 4][:], in_=HIDS[c, tb, :, :]
                                    ).then_inc(sdR[i % 4].h, 16)
                        sdR[i % 4].v += 16
                        assert sdR[i % 4].v == b_hid_mark(i)

            ps_o = [psum(f"pso{app}{i}", [128, TBS], F32, esb) for i in range(8)]

            @block.vector
            def _(v):
                for swp in range(NSW):
                    tb, s_ = swp // 2, swp % 2
                    v.wait_ge(st.h, b_mm_mark(swp * HC + HC - 1))
                    for m8 in range(8):
                        m = s_ * 8 + m8
                        sv.inc(v.scalar_tensor_tensor(
                            hT[:, hcol(m, tb * TBS, (tb + 1) * TBS)],
                            ps_o[m8][:], 0.5,
                            hT[:, hcol(m, tb * TBS, (tb + 1) * TBS)],
                            op0=ALU.mult, op1=ALU.add))
                    assert sv.v == b_evac_mark(swp)

            @block.tensor
            def _(t):
                for swp in range(NSW):
                    tb, s_ = swp // 2, swp % 2
                    for c in range(HC):
                        i = swp * HC + c
                        slot = i % 4
                        t.wait_ge(sdB[slot].h, b_load_mark(i))
                        t.wait_ge(sdR[i % 4].h, b_hid_mark(i))
                        if c == 0 and swp == 0:
                            t.wait_ge(ss.h, a_sig_mark(HC - 1, 1))
                            t.wait_ge(svh.h, a_hid_mark(HC - 1, 1))
                        for m8 in range(8):
                            if c == 0 and swp > 0:
                                t.wait_ge(sv.h, b_evac_bank_mark(swp - 1, m8))
                            mm = t.matmul(ps_o[m8][:],
                                          q3[slot][:, m8 * 128:(m8 + 1) * 128],
                                          hidl[i % 4][:],
                                          start=(c == 0), stop=(c == HC - 1))
                        st.inc(mm)
                        assert st.v == b_mm_mark(i)

            barrier()
            esb.close()
            esl.close()

        def emit_attn():
            esl = contextlib.ExitStack()
            kT = sb("kT", [128, NKV * 2048], BF16, esl)
            v_tm = sb("v_tm", [128, NKV * 2048], BF16, esl)
            wring = [sb(f"wring{i}", [128, 2048], BF16, esl) for i in range(2)]
            den_b = sb("den_b", [1, TBS], BF16, esl)
            den_f = sb("den_f", [1, TBS], F32, esl)
            negc_sb = sb("negc_sb", [128, 1], F32, esl)
            esvt = contextlib.ExitStack()
            vT = sb("vT", [128, NKV * 2048], BF16, esvt)
            sqa = sb("sqa", [128, 2048], BF16, esvt)

            ps_qp = psum("ps_qp", [128, T], F32, esl)

            sgq_b = [sgq[i].v for i in range(2)]
            st_a, ss_a, sv_a, sg_a = st.v, ss.v, sv.v, sg.v

            ring_src = [(WKL, 0), (WKL, 1), (WVL, 0), (WVL, 1)] \
                + [(WQL, i) for i in range(NH)] + [(WOL, i) for i in range(DC)]

            def ring_mark(i):
                return sgq_b[i % 2] + 16 * (i // 2 + 1)

            A0 = st_a
            order = []
            for jp in range(8):
                order.append(("sc", jp))
                if jp >= 1:
                    order.append(("pv", jp - 1))
            order.append(("pv", 7))
            sc_off, pv_off = [0] * 8, [0] * 8
            for idx_, (kind, jp) in enumerate(order):
                (sc_off if kind == "sc" else pv_off)[jp] = idx_ + 1

            def kv_mm_mark(oc, tb=1):
                return A0 + 2 * oc + tb + 1

            def vt_mark(grp):
                return A0 + 8 + grp + 1

            def kn_mm_mark(g_, k_):
                return A0 + 16 + 4 * g_ + k_ + 1

            QP0 = A0 + 24

            def qproj_mark(hh, tb=1):
                return QP0 + 4 * hh + tb + 1

            def qnA_mark(hh):
                return QP0 + 4 * hh + 3

            def qnB_mark(hh):
                return QP0 + 4 * hh + 4

            negc_bc_mark = QP0 + 65
            H0 = QP0 + 65

            def hb(idx):
                return H0 + 17 * idx

            def sc_mark(idx, jp):
                return hb(idx) + sc_off[jp]

            def pv_mark(idx, jp):
                return hb(idx) + pv_off[jp]

            def dbc_mark(idx):
                return hb(idx) + 17

            OP0 = H0 + 17 * 32

            def oproj_mark(m, tb):
                return OP0 + 2 * m + tb + 1

            SS0 = ss_a + 8

            def ss_qh_mark(hh, tb):
                return SS0 + 2 * hh + tb + 1

            ss_negc_mark = SS0 + 34

            def exp_mark(idx, jp):
                return SS0 + 34 + 10 * idx + jp + 1

            def ss_recip_mark(idx):
                return SS0 + 34 + 10 * idx + 10

            SV0 = sv_a + 8 + 13

            def qn_base(hh):
                return SV0 + 5 * hh

            NC0 = SV0 + 80
            sv_negc_mark = NC0 + 4
            HD0 = NC0 + 4

            def sv_rrep_mark(idx):
                return HD0 + 2 * idx + 1

            def sv_ao_mark(idx):
                return HD0 + 2 * idx + 2

            OE0 = HD0 + 64

            def sv_oevac_mark(m, tb):
                return OE0 + 2 * m + tb + 1

            def emit_ring(g, lo, hi):
                for i in range(lo, hi):
                    W, idx = ring_src[i]
                    slot = i % 2
                    if i >= 2:
                        j = i - 2
                        if j < 4:
                            g.wait_ge(st.h, kv_mm_mark(j))
                        elif j < 4 + NH:
                            g.wait_ge(st.h, qproj_mark(j - 4))
                        else:
                            g.wait_ge(st.h, oproj_mark(j - 4 - NH, 1))
                    g.dma_start(out=wring[slot][:], in_=W[idx, :, :]
                                ).then_inc(sgq[slot].h, 16)
                    sgq[slot].v += 16
                    assert sgq[slot].v == ring_mark(i)

            @block.gpsimd
            def _(g):
                emit_ring(g, 0, 4)

            @block.tensor
            def _(t):
                for oc in range(4):
                    slot = oc % 2
                    t.wait_ge(sgq[slot].h, ring_mark(oc))
                    for tb in range(2):
                        if oc >= 2:
                            t.wait_ge(ss.h, ss_a + 2 * (oc - 2) + tb + 1)
                        for ic in range(IC):
                            mm = t.matmul(ps_qp[:, tb * TBS:(tb + 1) * TBS],
                                          wring[slot][:, ic * 128:(ic + 1) * 128],
                                          nT[:, hcol(ic, tb * TBS, (tb + 1) * TBS)],
                                          start=(ic == 0), stop=(ic == IC - 1))
                        st.inc(mm)
                    assert st.v == kv_mm_mark(oc)

            @block.scalar
            def _(s):
                for oc in range(4):
                    g_, isv = oc % 2, oc // 2
                    dstT = vT if isv else kT
                    bias = (bv_sb if isv else bk_sb)[:, g_:g_ + 1]
                    for tb in range(2):
                        s.wait_ge(st.h, A0 + 2 * oc + tb + 1)
                        ss.inc(s.activation(
                            dstT[:, g_ * 2048 + tb * TBS:g_ * 2048 + (tb + 1) * TBS],
                            ps_qp[:, tb * TBS:(tb + 1) * TBS],
                            AF.Identity, bias=bias))
            assert ss.v == ss_a + 8

            @block.gpsimd
            def _(g):
                g.wait_ge(sd_b.h, 96)
                g.reg_load(rh, half_sb[:1, :1])
                g.reg_load(rn, nonce_sb[:1, :1])
                g.reg_mov(rit, 0)
                g.wait_ge(ss.h, ss_a + 8)
                for r in range(4):
                    isv, g_ = r // 2, r % 2
                    src = (vT if isv else kT)[:, g_ * 2048:g_ * 2048 + T]
                    with g.If_eq(rh, 0):
                        g.dma_start(out=XKV[0, r, :, :], in_=src).then_inc(sgkv.h, 16)
                    with g.Else():
                        g.dma_start(out=XKV[1, r, :, :], in_=src).then_inc(sgkv.h, 16)
                    sgkv.v += 16
                g.wait_ge(sgkv.h, sgkv.v)
                with g.If_eq(rh, 0):
                    g.dma_start(out=XKF[0:1, :], in_=nonce_sb[:]).then_inc(sgfl.h, 16)
                with g.Else():
                    g.dma_start(out=XKF[1:2, :], in_=nonce_sb[:]).then_inc(sgfl.h, 16)
                sgfl.v += 16
                g.wait_ge(sgfl.h, sgfl.v)
                g.reg_mov(rp, 1)
                with g.While(rp):
                    with g.If_eq(rh, 0):
                        g.dma_start(out=flag_sb[:], in_=XKF[1:2, :]).then_inc(sgpl.h, 16)
                    with g.Else():
                        g.dma_start(out=flag_sb[:], in_=XKF[0:1, :]).then_inc(sgpl.h, 16)
                    g.reg_add(rit, rit, 16)
                    g.wait_ge(sgpl.h, rit)
                    g.reg_load(rf, flag_sb[:1, :1])
                    g.reg_sub(rp, rf, rn)
                for r in range(4):
                    isv, g_ = r // 2, r % 2
                    dst = (vT if isv else kT)[:, g_ * 2048 + T:(g_ + 1) * 2048]
                    with g.If_eq(rh, 0):
                        g.dma_start(out=dst, in_=XKV[1, r, :, :]).then_inc(sgin.h, 16)
                    with g.Else():
                        g.dma_start(out=dst, in_=XKV[0, r, :, :]).then_inc(sgin.h, 16)
                    sgin.v += 16
            sgin_kv = sgin.v
            sgkv_out = sgkv.v

            @block.gpsimd
            def _(g):
                emit_ring(g, 4, len(ring_src))

            estr = contextlib.ExitStack()
            ps_tr = [psum(f"ps_tr{i}", [128, 512], BF16, estr) for i in range(2)]

            @block.tensor
            def _(t):
                t.wait_ge(sgin.h, sgin_kv)
                t.wait_ge(sgb.h, 16)
                for grp in range(8):
                    g_, jq = grp // 4, grp % 4
                    if grp >= 2:
                        t.wait_ge(sv.h, sv_a + grp - 1)
                    for jj in range(4):
                        j = 4 * jq + jj
                        tr = t.transpose(ps_tr[grp % 2][:, jj * 128:(jj + 1) * 128],
                                         vT[:, g_ * 2048 + j * 128:
                                            g_ * 2048 + (j + 1) * 128],
                                         idb[:])
                    st.inc(tr)
                    assert st.v == vt_mark(grp)

            @block.vector
            def _(v):
                for grp in range(8):
                    g_, jq = grp // 4, grp % 4
                    v.wait_ge(st.h, vt_mark(grp))
                    sv.inc(v.tensor_copy(
                        v_tm[:, g_ * 2048 + jq * 512:g_ * 2048 + (jq + 1) * 512],
                        ps_tr[grp % 2][:]))
            assert sv.v == sv_a + 8

            esvt.close()
            qT = sb("qT", [128, NH * T], BF16, esl)
            sqa1k = sb("sqa1k", [128, T], BF16, esl)
            aoT = sb("aoT", [128, DC * T], BF16, esl)
            probs = [sb(f"probs{i}", [128, T], BF16, esl) for i in range(3)]

            estr.close()
            ps_sc = [psum(f"ps_sc{i}", [128, T], F32, esl) for i in range(2)]
            ps_po = [psum(f"ps_po{i}", [128, TBS], F32, esl) for i in range(2)]

            def kn_sq_mark(g_):
                return sv_a + 8 + 6 * g_ + 1

            def kn_col_mark(g_, k_):
                return sv_a + 8 + 6 * g_ + 1 + k_ + 1

            @block.vector
            def _(v):
                for g_ in range(2):
                    v.wait_ge(sgin.h, sgin_kv)
                    if g_ == 1:
                        v.wait_ge(st.h, kn_mm_mark(0, 3))
                    sv.inc(v.tensor_mul(sqa[:], kT[:, g_ * 2048:(g_ + 1) * 2048],
                                        kT[:, g_ * 2048:(g_ + 1) * 2048]))
                    assert sv.v == kn_sq_mark(g_)
                    for k_ in range(4):
                        v.wait_ge(st.h, kn_mm_mark(g_, k_))
                        sv.inc(v.reduce_max(rows_sb[:, 44 + k_:45 + k_],
                                            ps_po[1][0:1, 0:TBS], axis=AX.X))
                        assert sv.v == kn_col_mark(g_, k_)
                    v.wait_ge(sv.h, sv.v)
                    sv.inc(v.reduce_max(kmax2_sb[:, g_:g_ + 1], rows_sb[:, 44:48],
                                        axis=AX.X))
                sv.inc(v.memset(rows_sb[:, 40:41], 0.0))
                assert sv.v == SV0

            @block.tensor
            def _(t):
                for g_ in range(2):
                    t.wait_ge(sv.h, kn_sq_mark(g_))
                    for k_ in range(4):
                        if (g_, k_) != (0, 0):
                            pk = (g_, k_ - 1) if k_ else (g_ - 1, 3)
                            t.wait_ge(sv.h, kn_col_mark(*pk))
                        st.inc(t.matmul(ps_po[1][0:1, 0:TBS], ones_col_b[:],
                                        sqa[:, k_ * TBS:(k_ + 1) * TBS],
                                        start=True, stop=True))
                        assert st.v == kn_mm_mark(g_, k_)


            @block.tensor
            def _(t):
                for hh in range(NH):
                    slot = (4 + hh) % 2
                    t.wait_ge(sgq[slot].h, ring_mark(4 + hh))
                    for tb in range(2):
                        if hh >= 1:
                            t.wait_ge(ss.h, ss_qh_mark(hh - 1, tb))
                        else:
                            t.wait_ge(ss.h, ss_a + 8)
                        for ic in range(IC):
                            mm = t.matmul(ps_qp[:, tb * TBS:(tb + 1) * TBS],
                                          wring[slot][:, ic * 128:(ic + 1) * 128],
                                          nT[:, hcol(ic, tb * TBS, (tb + 1) * TBS)],
                                          start=(ic == 0), stop=(ic == IC - 1))
                        st.inc(mm)
                        assert st.v == qproj_mark(hh, tb)
                    t.wait_ge(sv.h, qn_base(hh) + 1)
                    st.inc(t.matmul(ps_po[1][0:1, 0:TBS], ones_col_b[:],
                                    sqa1k[:, 0:TBS], start=True, stop=True))
                    t.wait_ge(sv.h, qn_base(hh) + 2)
                    st.inc(t.matmul(ps_po[1][0:1, 0:TBS], ones_col_b[:],
                                    sqa1k[:, TBS:2 * TBS], start=True, stop=True))
                    assert st.v == qnB_mark(hh)
                t.wait_ge(ss.h, ss_negc_mark)
                st.inc(t.matmul(ps_po[0][:, 0:1], negones_row_f[:],
                                rows_sb[:, 36:37], start=True, stop=True))
                assert st.v == negc_bc_mark

            @block.scalar
            def _(s):
                for hh in range(NH):
                    for tb in range(2):
                        s.wait_ge(st.h, qproj_mark(hh, tb))
                        if hh == 0 and tb == 0:
                            s.wait_ge(st.h, vt_mark(7))
                            s.wait_ge(sgkv.h, sgkv_out)
                        ss.inc(s.activation(qT[:, hh * T + tb * TBS:
                                               hh * T + (tb + 1) * TBS],
                                            ps_qp[:, tb * TBS:(tb + 1) * TBS],
                                            AF.Identity, bias=bq_sb[:, hh:hh + 1]))
                        assert ss.v == ss_qh_mark(hh, tb)
                s.wait_ge(sv.h, NC0 + 3)
                ss.inc(s.activation(rows_sb[:, 36:37], rows_sb[:, 38:39], AF.Ln))
                s.wait_ge(ss.h, ss.v)
                ss.inc(s.activation(rows_sb[:, 36:37], rows_sb[:, 36:37],
                                    AF.Exp, scale=0.5))
                assert ss.v == ss_negc_mark
                for idx in range(32):
                    hh, tb = idx // 2, idx % 2
                    for jp in range(8):
                        par = jp % 2
                        s.wait_ge(st.h, sc_mark(idx, jp))
                        if idx == 0 and jp == 0:
                            s.wait_ge(sv.h, sv_negc_mark)
                        gj = 8 * idx + jp
                        if gj >= 3:
                            pidx, pjp = divmod(gj - 3, 8)
                            s.wait_ge(st.h, pv_mark(pidx, pjp))
                        ss.inc(s.activation(probs[gj % 3][:], ps_sc[par][:],
                                            AF.Exp, bias=negc_sb[:, 0:1],
                                            scale=SCALE))
                        assert ss.v == exp_mark(idx, jp)
                    s.wait_ge(st.h, pv_mark(idx, 7))
                    ss.inc(s.activation(den_f[:], ps_qp[0:1, TBS:2 * TBS], AF.Ln))
                    s.wait_ge(ss.h, ss.v)
                    ss.inc(s.activation(den_b[:], den_f[:], AF.Exp, scale=-1.0))
                    assert ss.v == ss_recip_mark(idx)

            @block.vector
            def _(v):
                for hh in range(NH):
                    v.wait_ge(ss.h, ss_qh_mark(hh, 1))
                    if hh >= 1:
                        v.wait_ge(st.h, qnB_mark(hh - 1))
                    sv.inc(v.tensor_mul(sqa1k[:], qT[:, hh * T:(hh + 1) * T],
                                        qT[:, hh * T:(hh + 1) * T]))
                    assert sv.v == qn_base(hh) + 1
                    v.wait_ge(st.h, qnA_mark(hh))
                    sv.inc(v.reduce_max(rows_sb[:, 32:33], ps_po[1][0:1, 0:TBS],
                                        axis=AX.X))
                    v.wait_ge(st.h, qnB_mark(hh))
                    sv.inc(v.reduce_max(rows_sb[:, 33:34], ps_po[1][0:1, 0:TBS],
                                        axis=AX.X))
                    v.wait_ge(sv.h, sv.v)
                    sv.inc(v.tensor_max(rows_sb[:, 40:41], rows_sb[:, 40:41],
                                        rows_sb[:, 32:33]))
                    v.wait_ge(sv.h, sv.v)
                    sv.inc(v.tensor_max(rows_sb[:, 40:41], rows_sb[:, 40:41],
                                        rows_sb[:, 33:34]))
                    assert sv.v == qn_base(hh) + 5
                sv.inc(v.reduce_max(rows_sb[:, 41:42], kmax2_sb[:, 0:2], axis=AX.X))
                v.wait_ge(sv.h, sv.v)
                sv.inc(v.tensor_mul(rows_sb[:, 42:43], rows_sb[:, 40:41],
                                    rows_sb[:, 41:42]))
                v.wait_ge(sv.h, sv.v)
                sv.inc(v.tensor_single_scalar(rows_sb[:, 38:39], rows_sb[:, 42:43],
                                              1.0 / HD, op=ALU.mult))
                assert sv.v == NC0 + 3
                v.wait_ge(st.h, negc_bc_mark)
                sv.inc(v.tensor_copy(negc_sb[:], ps_po[0][:, 0:1]))
                assert sv.v == sv_negc_mark
                for idx in range(32):
                    hh, tb = idx // 2, idx % 2
                    v.wait_ge(st.h, dbc_mark(idx))
                    sv.inc(v.tensor_copy(rrep[:, 0:TBS], ps_sc[0][:, 0:TBS]))
                    assert sv.v == sv_rrep_mark(idx)
                    v.wait_ge(sv.h, sv.v)
                    sv.inc(v.tensor_mul(
                        aoT[:, hcol(hh, tb * TBS, (tb + 1) * TBS)],
                        ps_po[idx % 2][:], rrep[:, 0:TBS]))
                    assert sv.v == sv_ao_mark(idx)
                for m in range(DC):
                    for tb in range(2):
                        v.wait_ge(st.h, oproj_mark(m, tb))
                        sv.inc(v.scalar_tensor_tensor(
                            hT[:, hcol(m, tb * TBS, (tb + 1) * TBS)],
                            ps_qp[:, tb * TBS:(tb + 1) * TBS],
                            bo_sb[:, m:m + 1],
                            hT[:, hcol(m, tb * TBS, (tb + 1) * TBS)],
                            op0=ALU.add, op1=ALU.add))
                        assert sv.v == sv_oevac_mark(m, tb)

            @block.tensor
            def _(t):
                for idx in range(32):
                    hh, tb = idx // 2, idx % 2
                    g_ = hh // 8
                    for kind, jp in order:
                        if kind == "sc":
                            par = jp % 2
                            if jp >= 2:
                                t.wait_ge(ss.h, exp_mark(idx, jp - 2))
                            elif idx >= 1:
                                t.wait_ge(ss.h, exp_mark(idx - 1, 6 + jp))
                            if jp == 0:
                                t.wait_ge(ss.h, ss_qh_mark(hh, tb))
                                if idx >= 1:
                                    t.wait_ge(sv.h, sv_rrep_mark(idx - 1))
                            for jj in range(2):
                                j = 2 * jp + jj
                                mm = t.matmul(
                                    ps_sc[par][:, jj * TBS:(jj + 1) * TBS],
                                    kT[:, g_ * 2048 + j * 128:
                                       g_ * 2048 + (j + 1) * 128],
                                    qT[:, hh * T + tb * TBS:hh * T + (tb + 1) * TBS],
                                    start=True, stop=True)
                            st.inc(mm)
                            assert st.v == sc_mark(idx, jp)
                        else:
                            t.wait_ge(ss.h, exp_mark(idx, jp))
                            if jp == 0:
                                if idx == 0:
                                    t.wait_ge(sv.h, sv_negc_mark)
                                if idx >= 2:
                                    t.wait_ge(sv.h, sv_ao_mark(idx - 2))
                                if idx >= 1:
                                    t.wait_ge(ss.h, ss_recip_mark(idx - 1))
                            gslot = (8 * idx + jp) % 3
                            for jj in range(2):
                                j = 2 * jp + jj
                                t.matmul(ps_po[idx % 2][:],
                                         v_tm[:, g_ * 2048 + j * 128:
                                              g_ * 2048 + (j + 1) * 128],
                                         probs[gslot][:, jj * TBS:(jj + 1) * TBS],
                                         start=(jp == 0 and jj == 0),
                                         stop=(jp == 7 and jj == 1))
                            for jj in range(2):
                                mm = t.matmul(ps_qp[0:1, TBS:2 * TBS], ones_col_b[:],
                                              probs[gslot][:, jj * TBS:
                                                           (jj + 1) * TBS],
                                              start=(jp == 0 and jj == 0),
                                              stop=(jp == 7 and jj == 1))
                            st.inc(mm)
                            assert st.v == pv_mark(idx, jp)
                    t.wait_ge(ss.h, ss_recip_mark(idx))
                    st.inc(t.matmul(ps_sc[0][:, 0:TBS], ones_row_b[:],
                                    den_b[:], start=True, stop=True))
                    assert st.v == dbc_mark(idx)
                for m in range(DC):
                    slot = (4 + NH + m) % 2
                    t.wait_ge(sgq[slot].h, ring_mark(4 + NH + m))
                    if m == 0:
                        t.wait_ge(sv.h, sv_ao_mark(31))
                        t.wait_ge(ss.h, ss_qh_mark(NH - 1, 1))
                    for tb in range(2):
                        if m >= 1:
                            t.wait_ge(sv.h, sv_oevac_mark(m - 1, tb))
                        for ac in range(IC):
                            mm = t.matmul(ps_qp[:, tb * TBS:(tb + 1) * TBS],
                                          wring[slot][:, ac * 128:(ac + 1) * 128],
                                          aoT[:, hcol(ac, tb * TBS, (tb + 1) * TBS)],
                                          start=(ac == 0), stop=(ac == IC - 1))
                        st.inc(mm)
                        assert st.v == oproj_mark(m, tb)

            barrier()
            esl.close()

        emit_ln("1")
        esg.close()
        emit_ffn(1)
        emit_ln("2")
        emit_attn()
        emit_ln("3")
        emit_ffn(2)

        @block.sync
        def _(q):
            for c in range(DC):
                q.dma_start(out=OUT[c * 128:(c + 1) * 128, :], in_=hT[:, hcol(c)]
                            ).then_inc(sd_out.h, 16)
            sd_out.v += 16 * DC
            q.wait_ge(sd_out.h, sd_out.v)

    es.close()
    return nc



def _layout_w(w):
    O, I = w.shape
    return np.ascontiguousarray(
        w.reshape(O // 128, 128, I // 128, 128).transpose(0, 3, 2, 1)
    ).reshape(O // 128, 128, I)


def kernel(L0_attn_x, L0_attn_w_q, L0_attn_b_q, L0_attn_w_k, L0_attn_b_k,
           L0_attn_w_v, L0_attn_b_v, L0_attn_w_o, L0_attn_b_o,
           L0_ffn_w1, L0_ffn_w2, L0_ffn_w3):
    x = np.asarray(L0_attn_x, dtype=np.float32)
    W1 = _layout_w(np.asarray(L0_ffn_w1, np.float32))
    W2 = _layout_w(np.asarray(L0_ffn_w2, np.float32))
    W3 = np.ascontiguousarray(np.asarray(L0_ffn_w3, np.float32).T
                              ).reshape(HC, 128, 2048)
    WQ = _layout_w(np.asarray(L0_attn_w_q, np.float32))
    WK = _layout_w(np.asarray(L0_attn_w_k, np.float32))
    WV = _layout_w(np.asarray(L0_attn_w_v, np.float32))
    WO = _layout_w(np.asarray(L0_attn_w_o, np.float32))
    BQa = np.ascontiguousarray(np.asarray(L0_attn_b_q, np.float32).reshape(16, 128).T)
    BKa = np.ascontiguousarray(np.asarray(L0_attn_b_k, np.float32).reshape(2, 128).T)
    BVa = np.ascontiguousarray(np.asarray(L0_attn_b_v, np.float32).reshape(2, 128).T)
    BOa = np.ascontiguousarray(np.asarray(L0_attn_b_o, np.float32).reshape(16, 128).T)
    IDENTa = np.eye(128, dtype=np.float32)
    nonce = np.array([[np.random.randint(1, 2 ** 30)]], np.int32)

    in_maps = []
    for core in range(8):
        b, h = core // 2, core % 2
        xt = np.ascontiguousarray(x[b, h * 1024:(h + 1) * 1024, :].T)
        in_maps.append({
            "XT": xt, "W1L": W1, "W2L": W2, "W3L": W3,
            "WQL": WQ, "WKL": WK, "WVL": WV, "WOL": WO,
            "BQ": BQa, "BK": BKa, "BV": BVa, "BO": BOa,
            "IDENT": IDENTa,
            "HALF": np.array([[h]], np.int32), "NONCE": nonce,
        })
    nc = build_program()
    res = run_bass_kernel_spmd(nc, in_maps, core_ids=list(range(8)))
    out = np.empty((4, 2048, 2048), np.float32)
    for core in range(8):
        b, h = core // 2, core % 2
        out[b, h * 1024:(h + 1) * 1024, :] = np.asarray(res.results[core]["OUT"]).T
    return out


# version 11
# speedup vs baseline: 1.2504x; 1.2504x over previous
import contextlib
import numpy as np
import concourse.bass as bass
import concourse.mybir as mybir
from concourse.bass_utils import run_bass_kernel_spmd

F32 = mybir.dt.float32
BF16 = mybir.dt.bfloat16
I32 = mybir.dt.int32
AF = mybir.ActivationFunctionType
ALU = mybir.AluOpType
AX = mybir.AxisListType

NB, SEQ, D, HID = 4, 2048, 2048, 5632
T = 1024
TBS = 512
DC = D // 128
IC = 16
HC = HID // 128
NH, NKV, HD = 16, 2, 128
SCALE = 1.0 / float(np.sqrt(np.float64(HD)))
NW = HID * D
EPS_Q = 1e-8
EPS_LN = 1e-5


class Cnt:
    REG = []

    def __init__(self, es, nc, name):
        self.h = es.enter_context(nc.semaphore(name))
        self.v = 0
        self.name = name
        Cnt.REG.append(self)

    def inc(self, inst, k=1):
        inst.then_inc(self.h, k)
        self.v += k
        return self.v


def build_program():
    Cnt.REG = []
    nc = bass.Bass("TRN2", target_bir_lowering=False,
                   detect_race_conditions=False)

    XT = nc.dram_tensor("XT", [D, T], F32, kind="ExternalInput").ap()
    W1L = nc.dram_tensor("W1L", [HC, 128, 2048], F32, kind="ExternalInput").ap()
    W2L = nc.dram_tensor("W2L", [HC, 128, 2048], F32, kind="ExternalInput").ap()
    W3L = nc.dram_tensor("W3L", [HC, 128, 2048], F32, kind="ExternalInput").ap()
    WQL = nc.dram_tensor("WQL", [NH, 128, 2048], F32, kind="ExternalInput").ap()
    WKL = nc.dram_tensor("WKL", [NKV, 128, 2048], F32, kind="ExternalInput").ap()
    WVL = nc.dram_tensor("WVL", [NKV, 128, 2048], F32, kind="ExternalInput").ap()
    WOL = nc.dram_tensor("WOL", [DC, 128, 2048], F32, kind="ExternalInput").ap()
    BQ = nc.dram_tensor("BQ", [128, NH], F32, kind="ExternalInput").ap()
    BK = nc.dram_tensor("BK", [128, NKV], F32, kind="ExternalInput").ap()
    BV = nc.dram_tensor("BV", [128, NKV], F32, kind="ExternalInput").ap()
    BO = nc.dram_tensor("BO", [128, DC], F32, kind="ExternalInput").ap()
    IDENT = nc.dram_tensor("IDENT", [128, 128], F32, kind="ExternalInput").ap()
    HALF = nc.dram_tensor("HALF", [1, 1], I32, kind="ExternalInput").ap()
    NONCE = nc.dram_tensor("NONCE", [1, 1], I32, kind="ExternalInput").ap()
    OUT = nc.dram_tensor("OUT", [D, T], F32, kind="ExternalOutput").ap()

    Q12S = nc.dram_tensor("Q12S", [HC, 2, 128, 2048], BF16, kind="Internal").ap()
    Q3S = nc.dram_tensor("Q3S", [HC, 128, 2048], BF16, kind="Internal").ap()
    HIDS = nc.dram_tensor("HIDS", [HC, 2, 128, TBS], BF16, kind="Internal").ap()
    XKV = nc.dram_tensor("XKV", [2, 4, 128, T], BF16, kind="Internal",
                         addr_space="Shared").ap()
    XG = nc.dram_tensor("XG", [2, 4], F32, kind="Internal", addr_space="Shared").ap()
    XGF = nc.dram_tensor("XGF", [2, 16], I32, kind="Internal", addr_space="Shared").ap()
    XKF = nc.dram_tensor("XKF", [2, 16], I32, kind="Internal", addr_space="Shared").ap()

    es = contextlib.ExitStack()

    def sb(name, shape, dtype, st_=None):
        return (st_ or es).enter_context(nc.sbuf_tensor(name, shape, dtype))

    def psum(name, shape, dtype=F32, st_=None):
        return (st_ or es).enter_context(nc.psum_tensor(name, shape, dtype))

    hT = sb("hT", [128, DC * T], F32)
    nT = sb("nT", [128, DC * T], BF16)
    rrep = sb("rrep", [128, T], F32)
    ones_col_f = sb("ones_col_f", [128, 1], F32)
    ones_col_b = sb("ones_col_b", [128, 1], BF16)
    ones_row_f = sb("ones_row_f", [1, 128], F32)
    ones_row_b = sb("ones_row_b", [1, 128], BF16)
    negones_row_f = sb("negones_row_f", [1, 128], F32)
    bq_sb = sb("bq_sb", [128, NH], F32)
    bk_sb = sb("bk_sb", [128, NKV], F32)
    bv_sb = sb("bv_sb", [128, NKV], F32)
    bo_sb = sb("bo_sb", [128, DC], F32)
    idb = sb("idb", [128, 128], BF16)
    half_sb = sb("half_sb", [1, 1], I32)
    nonce_sb = sb("nonce_sb", [1, 16], I32)
    flag_sb = sb("flag_sb", [1, 16], I32)
    thr_sb = sb("thr_sb", [128, 8], F32)
    rows_sb = sb("rows_sb", [1, 64], F32)
    g3row = sb("g3row", [1, 4], F32)
    growB = sb("growB", [1, 4], F32)
    kmax2_sb = sb("kmax2_sb", [1, 2], F32)
    bump_sb = sb("bump_sb", [1, 8], F32)

    rh = es.enter_context(nc.gpsimd.register("rh"))
    rn = es.enter_context(nc.gpsimd.register("rn"))
    rf = es.enter_context(nc.gpsimd.register("rf"))
    rp = es.enter_context(nc.gpsimd.register("rp"))
    rit = es.enter_context(nc.gpsimd.register("rit"))
    qrh = es.enter_context(nc.sync.register("qrh"))

    sv = Cnt(es, nc, "sv")
    svh = Cnt(es, nc, "svh")
    ss = Cnt(es, nc, "ss")
    st = Cnt(es, nc, "st")
    sg = Cnt(es, nc, "sg")
    sd_x = Cnt(es, nc, "sd_x")
    sd_b = Cnt(es, nc, "sd_b")
    sd_out = Cnt(es, nc, "sd_out")
    sdG = [Cnt(es, nc, f"sdG{i}") for i in range(4)]
    sdA = [Cnt(es, nc, f"sdA{i}") for i in range(2)]
    sdB = [Cnt(es, nc, f"sdB{i}") for i in range(4)]
    sdS = [Cnt(es, nc, f"sdS{i}") for i in range(2)]
    sdT = [Cnt(es, nc, f"sdT{i}") for i in range(2)]
    sdH = [Cnt(es, nc, f"sdH{i}") for i in range(4)]
    sdR = [Cnt(es, nc, f"sdR{i}") for i in range(4)]
    sgq = [Cnt(es, nc, f"sgq{i}") for i in range(2)]
    sgkv = Cnt(es, nc, "sgkv")
    sgfl = Cnt(es, nc, "sgfl")
    sgpl = Cnt(es, nc, "sgpl")
    sgin = Cnt(es, nc, "sgin")
    sgb = Cnt(es, nc, "sgb")

    thr_marks = {}

    def hcol(c, t0=0, t1=T):
        return slice(c * T + t0, c * T + t1)

    with nc.Block() as block:

        def barrier():
            snap = [(c, c.v) for c in Cnt.REG if c.v > 0]

            def waits(q, own):
                for c, v in snap:
                    if c is not own:
                        q.wait_ge(c.h, v)

            @block.sync
            def _(q):
                waits(q, None)

            @block.vector
            def _(v):
                waits(v, sv)

            @block.scalar
            def _(s):
                waits(s, ss)

            @block.tensor
            def _(t):
                waits(t, st)

            @block.gpsimd
            def _(g):
                waits(g, sg)

        @block.sync
        def _(q):
            for c in range(DC):
                q.dma_start(out=hT[:, hcol(c)], in_=XT[c * 128:(c + 1) * 128, :]
                            ).then_inc(sd_x.h, 16)
            sd_x.v += 16 * DC
            for dst, src in ((bq_sb, BQ), (bk_sb, BK), (bv_sb, BV), (bo_sb, BO)):
                q.dma_start(out=dst[:], in_=src).then_inc(sd_b.h, 16)
            q.dma_start(out=half_sb[:], in_=HALF).then_inc(sd_b.h, 16)
            q.dma_start(out=nonce_sb[:, 0:1], in_=NONCE).then_inc(sd_b.h, 16)
            sd_b.v += 16 * 6

        @block.gpsimd
        def _(g):
            g.dma_start(out=idb[:], in_=IDENT).then_inc(sgb.h, 16)
            sgb.v += 16

        @block.vector
        def _(v):
            v.memset(ones_col_f[:], 1.0)
            v.memset(ones_col_b[:], 1.0)
            v.memset(ones_row_f[:], 1.0)
            v.memset(ones_row_b[:], 1.0)
            v.memset(negones_row_f[:], -1.0)
            v.memset(g3row[:], 0.0)
            v.memset(growB[:], 0.0)
            v.memset(nonce_sb[:, 1:16], 0)
            sv.inc(v.memset(bump_sb[:], 0.0))

        esg = contextlib.ExitStack()
        gsl = [sb(f"gsl{i}", [128, 2048], F32, esg) for i in range(4)]
        gpart = sb("gpart", [128, 144], F32, esg)
        gdump = sb("gdump", [128, 2048], F32, esg)
        ps_g3 = psum("ps_g3", [1, 4], F32, esg)
        ps_thr = psum("ps_thr", [128, 8], F32, esg)
        NG = 3 * HC

        sv_g_base = sv.v
        ss_g_base = ss.v

        def g_red_mark(i):
            if i >= 2 * HC:
                return ss_g_base + HC + (i - 2 * HC) + 1
            if i % 2 == 0:
                return sv_g_base + i // 2 + 1
            return ss_g_base + i // 2 + 1

        @block.gpsimd
        def _(g):
            for i in range(NG):
                slot = i % 4
                mat, j = i // HC, i % HC
                W = (W1L, W2L, W3L)[mat]
                if i >= 4:
                    ip = i - 4
                    if ip < 2 * HC and ip % 2 == 0:
                        g.wait_ge(sv.h, g_red_mark(ip))
                    else:
                        g.wait_ge(ss.h, g_red_mark(ip))
                g.dma_start(out=gsl[slot][:], in_=W[j, :, :]
                            ).then_inc(sdG[slot].h, 16)
                sdG[slot].v += 16

        @block.vector
        def _(v):
            for i in range(0, 2 * HC, 2):
                v.wait_ge(sdG[i % 4].h, 16 * (i // 4 + 1))
                sv.inc(v.reduce_sum(gpart[:, i:i + 1], gsl[i % 4][:], axis=AX.X,
                                    apply_absolute_value=True))
                assert sv.v == g_red_mark(i)
            v.wait_ge(ss.h, g_red_mark(2 * HC - 1))
            for m in range(2):
                v.wait_ge(sv.h, sv.v)
                sv.inc(v.reduce_sum(gpart[:, 136 + m:137 + m],
                                    gpart[:, m * HC:(m + 1) * HC], axis=AX.X))
        sv_gpart12 = sv.v

        @block.scalar
        def _(s):
            for i in range(1, 2 * HC, 2):
                s.wait_ge(sdG[i % 4].h, 16 * (i // 4 + 1))
                ss.inc(s.activation(gdump[:], gsl[i % 4][:], AF.Abs,
                                    accum_out=gpart[:, i:i + 1]))
                assert ss.v == g_red_mark(i)

        @block.tensor
        def _(t):
            t.wait_ge(sv.h, sv_gpart12)
            for m in range(2):
                mm = t.matmul(ps_g3[:, m:m + 1], gpart[:, 136 + m:137 + m],
                              ones_col_f[:], start=True, stop=True)
            st.inc(mm)
        st_g12 = st.v

        @block.vector
        def _(v):
            v.wait_ge(st.h, st_g12)
            sv.inc(v.tensor_copy(g3row[:, 0:2], ps_g3[:, 0:2]))
            v.wait_ge(sv.h, sv.v)
            v.tensor_scalar(rows_sb[:, 8:10], g3row[:, 0:2],
                            0.5 / NW, 0.5 * EPS_Q, op0=ALU.mult, op1=ALU.add)
            sv.inc(v.tensor_scalar(rows_sb[:, 11:13], g3row[:, 0:2],
                                   -0.5 / NW, -0.5 * EPS_Q, op0=ALU.mult, op1=ALU.add))
        sv_throw12 = sv.v

        @block.tensor
        def _(t):
            t.wait_ge(sv.h, sv_throw12)
            st.inc(t.matmul(ps_thr[:, 0:2], ones_row_f[:], rows_sb[:, 8:10],
                            start=True, stop=True))
            st.inc(t.matmul(ps_thr[:, 3:5], ones_row_f[:], rows_sb[:, 11:13],
                            start=True, stop=True))
        st_thr12 = st.v

        @block.vector
        def _(v):
            v.wait_ge(st.h, st_thr12)
            sv.inc(v.tensor_copy(thr_sb[:, 0:2], ps_thr[:, 0:2]))
            sv.inc(v.tensor_copy(thr_sb[:, 3:5], ps_thr[:, 3:5]))
        sv_thr12 = sv.v

        @block.scalar
        def _(s):
            for i in range(2 * HC, NG):
                s.wait_ge(sdG[i % 4].h, 16 * (i // 4 + 1))
                ss.inc(s.activation(gdump[:], gsl[i % 4][:], AF.Abs,
                                    accum_out=gpart[:, i:i + 1]))
                assert ss.v == g_red_mark(i)
        ss_gpart3 = ss.v

        def emit_gamma3_tail():
            @block.vector
            def _(v):
                v.wait_ge(ss.h, ss_gpart3)
                sv.inc(v.reduce_sum(gpart[:, 138:139], gpart[:, 2 * HC:NG],
                                    axis=AX.X))
            globals()
            return sv.v

        def emit_gamma3_thr():
            sv_gpart3 = emit_gamma3_tail()

            @block.tensor
            def _(t):
                t.wait_ge(sv.h, sv_gpart3)
                st.inc(t.matmul(ps_g3[:, 2:3], gpart[:, 138:139],
                                ones_col_f[:], start=True, stop=True))
            st_g3 = st.v

            @block.vector
            def _(v):
                v.wait_ge(st.h, st_g3)
                sv.inc(v.tensor_copy(g3row[:, 2:3], ps_g3[:, 2:3]))
                v.wait_ge(sv.h, sv.v)
                v.tensor_scalar(rows_sb[:, 10:11], g3row[:, 2:3],
                                0.5 / NW, 0.5 * EPS_Q, op0=ALU.mult, op1=ALU.add)
                sv.inc(v.tensor_scalar(rows_sb[:, 13:14], g3row[:, 2:3],
                                       -0.5 / NW, -0.5 * EPS_Q,
                                       op0=ALU.mult, op1=ALU.add))
            sv_throw3 = sv.v

            @block.tensor
            def _(t):
                t.wait_ge(sv.h, sv_throw3)
                st.inc(t.matmul(ps_thr[:, 2:3], ones_row_f[:], rows_sb[:, 10:11],
                                start=True, stop=True))
                st.inc(t.matmul(ps_thr[:, 5:6], ones_row_f[:], rows_sb[:, 13:14],
                                start=True, stop=True))
            st_thr3 = st.v

            @block.vector
            def _(v):
                v.wait_ge(st.h, st_thr3)
                sv.inc(v.tensor_copy(thr_sb[:, 2:3], ps_thr[:, 2:3]))
                sv.inc(v.tensor_copy(thr_sb[:, 5:6], ps_thr[:, 5:6]))
            thr_marks["thr12"] = sv_thr12
            thr_marks["thr3"] = sv.v

        def emit_ln(tag):
            esl = contextlib.ExitStack()
            sqr = [sb(f"sqr{tag}{i}", [128, T], BF16, esl) for i in range(2)]
            ln_tmp = sb(f"ln_tmp{tag}", [128, T], F32, esl)
            brep = sb(f"brep{tag}", [128, T], F32, esl)
            lnrow = sb(f"lnrow{tag}", [1, 6 * TBS], F32, esl)
            ps_sum = [psum(f"ps_sum{tag}{i}", [1, TBS], F32, esl) for i in range(2)]
            ps_sq = [psum(f"ps_sq{tag}{i}", [1, TBS], F32, esl) for i in range(2)]
            ps_bc = psum(f"ps_bc{tag}", [128, T], F32, esl)

            sv_base, st_base = sv.v, st.v

            def cast_mark(c):
                return sv_base + c + 1

            def stats_mark(c):
                return st_base + 4 * (c + 1)

            @block.vector
            def _(v):
                for c in range(DC):
                    if c >= 2:
                        v.wait_ge(st.h, stats_mark(c - 2))
                    v.tensor_copy(nT[:, hcol(c)], hT[:, hcol(c)])
                    v.wait_ge(sv.h, sv.v)
                    sv.inc(v.tensor_mul(sqr[c % 2][:], nT[:, hcol(c)], nT[:, hcol(c)]))
                    assert sv.v == cast_mark(c)

            @block.tensor
            def _(t):
                for c in range(DC):
                    t.wait_ge(sv.h, cast_mark(c))
                    for tb in range(2):
                        st.inc(t.matmul(ps_sum[tb][:], ones_col_b[:],
                                        nT[:, hcol(c, tb * TBS, (tb + 1) * TBS)],
                                        start=(c == 0), stop=(c == DC - 1)))
                        st.inc(t.matmul(ps_sq[tb][:], ones_col_b[:],
                                        sqr[c % 2][:, tb * TBS:(tb + 1) * TBS],
                                        start=(c == 0), stop=(c == DC - 1)))
                    assert st.v == stats_mark(c)
            st_stats = st.v

            @block.vector
            def _(v):
                v.wait_ge(st.h, st_stats)
                for tb in range(2):
                    o = 3 * tb * TBS
                    sv.inc(v.tensor_single_scalar(lnrow[:, o:o + TBS], ps_sum[tb][:],
                                                  1.0 / D, op=ALU.mult))
                    v.wait_ge(sv.h, sv.v)
                    sv.inc(v.tensor_mul(lnrow[:, o + TBS:o + 2 * TBS],
                                        lnrow[:, o:o + TBS], lnrow[:, o:o + TBS]))
                    sv.inc(v.tensor_scalar(lnrow[:, o + 2 * TBS:o + 3 * TBS],
                                           ps_sq[tb][:], 1.0 / D, EPS_LN,
                                           op0=ALU.mult, op1=ALU.add))
                    v.wait_ge(sv.h, sv.v)
                    sv.inc(v.tensor_sub(lnrow[:, o + 2 * TBS:o + 3 * TBS],
                                        lnrow[:, o + 2 * TBS:o + 3 * TBS],
                                        lnrow[:, o + TBS:o + 2 * TBS]))
            sv_var = sv.v

            @block.scalar
            def _(s):
                s.wait_ge(sv.h, sv_var)
                for tb in range(2):
                    o = 3 * tb * TBS
                    ss.inc(s.activation(lnrow[:, o + 2 * TBS:o + 3 * TBS],
                                        lnrow[:, o + 2 * TBS:o + 3 * TBS], AF.Ln))
                    s.wait_ge(ss.h, ss.v)
                    ss.inc(s.activation(lnrow[:, o + 2 * TBS:o + 3 * TBS],
                                        lnrow[:, o + 2 * TBS:o + 3 * TBS],
                                        AF.Exp, scale=-0.5))
            ss_r = ss.v

            @block.vector
            def _(v):
                v.wait_ge(ss.h, ss_r)
                for tb in range(2):
                    o = 3 * tb * TBS
                    sv.inc(v.scalar_tensor_tensor(
                        lnrow[:, o:o + TBS], lnrow[:, o:o + TBS], -1.0,
                        lnrow[:, o + 2 * TBS:o + 3 * TBS],
                        op0=ALU.mult, op1=ALU.mult))
            sv_rows = sv.v

            @block.tensor
            def _(t):
                t.wait_ge(sv.h, sv_rows)
                for tb in range(2):
                    o = 3 * tb * TBS
                    st.inc(t.matmul(ps_bc[:, tb * TBS:(tb + 1) * TBS], ones_row_f[:],
                                    lnrow[:, o + 2 * TBS:o + 3 * TBS],
                                    start=True, stop=True))
            st_bc1 = st.v

            @block.vector
            def _(v):
                v.wait_ge(st.h, st_bc1)
                sv.inc(v.tensor_copy(rrep[:], ps_bc[:]))
            sv_rrep = sv.v

            @block.tensor
            def _(t):
                t.wait_ge(sv.h, sv_rrep)
                for tb in range(2):
                    o = 3 * tb * TBS
                    st.inc(t.matmul(ps_bc[:, tb * TBS:(tb + 1) * TBS], ones_row_f[:],
                                    lnrow[:, o:o + TBS], start=True, stop=True))
            st_bc2 = st.v

            @block.vector
            def _(v):
                v.wait_ge(st.h, st_bc2)
                sv.inc(v.tensor_copy(brep[:], ps_bc[:]))
                v.wait_ge(sv.h, sv.v)
                for c in range(DC):
                    sv.inc(v.tensor_mul(ln_tmp[:], hT[:, hcol(c)], rrep[:]))
                    v.wait_ge(sv.h, sv.v)
                    sv.inc(v.tensor_add(nT[:, hcol(c)], ln_tmp[:], brep[:]))
                    v.wait_ge(sv.h, sv.v)

            barrier()
            esl.close()

        def emit_ffn(app):
            esl = contextlib.ExitStack()
            w12raw = [sb(f"w12r{app}{i}", [128, 2048], F32, esl) for i in range(2)]
            q12 = [sb(f"q12_{app}{i}", [128, 4096], BF16, esl) for i in range(2)]
            maskr = [sb(f"mask{app}{i}", [128, 1024], BF16, esl) for i in range(4)]
            if app == 1:
                w3A = [sb(f"w3A{i}", [128, 2048], F32, esl) for i in range(2)]
                q3A = [sb(f"q3A{i}", [128, 2048], BF16, esl) for i in range(2)]
                mask3 = [sb(f"mask3{i}", [128, 2048], BF16, esl) for i in range(2)]
            sig = [sb(f"sig{app}{i}", [128, TBS], BF16, esl) for i in range(4)]
            hid_ring = [sb(f"hidr{app}{i}", [128, TBS], BF16, esl) for i in range(4)]

            esa = contextlib.ExitStack()
            ps_x1 = [psum(f"psx1{app}{i}", [128, TBS], F32, esa) for i in range(4)]
            ps_x2 = [psum(f"psx2{app}{i}", [128, TBS], F32, esa) for i in range(4)]

            NU = 2 * HC
            sdA_b = [sdA[i].v for i in range(2)]
            sdS_b = [sdS[i].v for i in range(2)]
            sdH_b = [sdH[i].v for i in range(4)]
            sg_b, svq_b, ss_b, st_b = sg.v, sv.v, ss.v, st.v
            svh_b = svh.v
            sdB_bA = [sdB[i].v for i in range(2)]
            sdT_bA = [sdT[i].v for i in range(2)]

            def a_load_mark(u):
                return sdA_b[u % 2] + 32 * (u // 2 + 1)

            def a2_load_mark(c):
                return sdA_b[c % 2] + 32 * (c // 2 + 1)

            def a_quant_mark(u):
                c, hh = u // 2, u % 2
                return svq_b + 10 * c + 4 * (hh + 1)

            def a_w3q_mark(c):
                return svq_b + 10 * (c + 1)

            def a_w3load_mark(c):
                return sdB_bA[c % 2] + 16 * (c // 2 + 1)

            def a_w3scr_mark(c):
                return sdT_bA[c % 2] + 16 * (c // 2 + 1)

            def a_scr_mark(u):
                return sdS_b[u % 2] + 32 * (u // 2 + 1)

            def a_mm_mark(c):
                return st_b + 4 * (c + 1)

            def a_sig_mark(c, tb):
                return ss_b + 2 * c + tb + 1

            def a_hid_mark(c, tb):
                return svh_b + 2 * c + tb + 1

            def a_spill_mark(c, tb):
                k = 2 * c + tb
                return sdH_b[k % 4] + 16 * (k // 4 + 1)

            @block.sync
            def _(q):
                if app == 1:
                    for u in range(NU):
                        slot, c, hh = u % 2, u // 2, u % 2
                        if u >= 2:
                            q.wait_ge(sv.h, a_quant_mark(u - 2))
                        q.dma_start(out=w12raw[slot][:, 0:1024],
                                    in_=W1L[c, :, hh * 1024:(hh + 1) * 1024]
                                    ).then_inc(sdA[slot].h, 16)
                        q.dma_start(out=w12raw[slot][:, 1024:2048],
                                    in_=W2L[c, :, hh * 1024:(hh + 1) * 1024]
                                    ).then_inc(sdA[slot].h, 16)
                        sdA[slot].v += 32
                        assert sdA[slot].v == a_load_mark(u)
                        if hh == 0:
                            if c >= 2:
                                q.wait_ge(sv.h, a_w3q_mark(c - 2))
                            q.dma_start(out=w3A[c % 2][:], in_=W3L[c, :, :]
                                        ).then_inc(sdB[c % 2].h, 16)
                            sdB[c % 2].v += 16
                            assert sdB[c % 2].v == a_w3load_mark(c)
                        else:
                            if c >= 2:
                                cq = c - 2
                                q.wait_ge(sv.h, a_w3q_mark(cq))
                                q.dma_start(out=Q3S[cq, :, :], in_=q3A[cq % 2][:]
                                            ).then_inc(sdT[cq % 2].h, 16)
                                sdT[cq % 2].v += 16
                                assert sdT[cq % 2].v == a_w3scr_mark(cq)
                        if u >= 2:
                            up = u - 2
                            ps_, cp, hp = up % 2, up // 2, up % 2
                            q.wait_ge(sv.h, a_quant_mark(up))
                            q.dma_start(out=Q12S[cp, 0, :, hp * 1024:(hp + 1) * 1024],
                                        in_=q12[cp % 2][:, hp * 1024:(hp + 1) * 1024]
                                        ).then_inc(sdS[ps_].h, 16)
                            q.dma_start(out=Q12S[cp, 1, :, hp * 1024:(hp + 1) * 1024],
                                        in_=q12[cp % 2][:, 2048 + hp * 1024:
                                                       2048 + (hp + 1) * 1024]
                                        ).then_inc(sdS[ps_].h, 16)
                            sdS[ps_].v += 32
                            assert sdS[ps_].v == a_scr_mark(up)
                        ku = u - 8
                        if ku >= 0:
                            cs, tbs_ = ku // 2, ku % 2
                            q.wait_ge(svh.h, a_hid_mark(cs, tbs_))
                            q.dma_start(out=HIDS[cs, tbs_, :, :],
                                        in_=hid_ring[ku % 4][:]).then_inc(sdH[ku % 4].h, 16)
                            sdH[ku % 4].v += 16
                            assert sdH[ku % 4].v == a_spill_mark(cs, tbs_)
                    for cq in (HC - 2, HC - 1):
                        q.wait_ge(sv.h, a_w3q_mark(cq))
                        q.dma_start(out=Q3S[cq, :, :], in_=q3A[cq % 2][:]
                                    ).then_inc(sdT[cq % 2].h, 16)
                        sdT[cq % 2].v += 16
                        assert sdT[cq % 2].v == a_w3scr_mark(cq)
                    for up in range(NU - 2, NU):
                        ps_, cp, hp = up % 2, up // 2, up % 2
                        q.wait_ge(sv.h, a_quant_mark(up))
                        q.dma_start(out=Q12S[cp, 0, :, hp * 1024:(hp + 1) * 1024],
                                    in_=q12[cp % 2][:, hp * 1024:(hp + 1) * 1024]
                                    ).then_inc(sdS[ps_].h, 16)
                        q.dma_start(out=Q12S[cp, 1, :, hp * 1024:(hp + 1) * 1024],
                                    in_=q12[cp % 2][:, 2048 + hp * 1024:
                                                   2048 + (hp + 1) * 1024]
                                    ).then_inc(sdS[ps_].h, 16)
                        sdS[ps_].v += 32
                        assert sdS[ps_].v == a_scr_mark(up)
                    for ku in range(NU - 8, NU):
                        cs, tbs_ = ku // 2, ku % 2
                        q.wait_ge(svh.h, a_hid_mark(cs, tbs_))
                        q.dma_start(out=HIDS[cs, tbs_, :, :],
                                    in_=hid_ring[ku % 4][:]).then_inc(sdH[ku % 4].h, 16)
                        sdH[ku % 4].v += 16
                        assert sdH[ku % 4].v == a_spill_mark(cs, tbs_)
                else:
                    for c in range(HC):
                        slot = c % 2
                        if c >= 2:
                            q.wait_ge(st.h, a_mm_mark(c - 2))
                        q.dma_start(out=q12[slot][:, 0:2048], in_=Q12S[c, 0, :, :]
                                    ).then_inc(sdA[slot].h, 16)
                        q.dma_start(out=q12[slot][:, 2048:4096], in_=Q12S[c, 1, :, :]
                                    ).then_inc(sdA[slot].h, 16)
                        sdA[slot].v += 32
                        assert sdA[slot].v == a2_load_mark(c)
                        ku = 2 * c - 8
                        for k in (ku, ku + 1):
                            if 0 <= k:
                                cs, tbs_ = k // 2, k % 2
                                q.wait_ge(svh.h, a_hid_mark(cs, tbs_))
                                q.dma_start(out=HIDS[cs, tbs_, :, :],
                                            in_=hid_ring[k % 4][:]
                                            ).then_inc(sdH[k % 4].h, 16)
                                sdH[k % 4].v += 16
                                assert sdH[k % 4].v == a_spill_mark(cs, tbs_)
                    for k in range(2 * HC - 8, 2 * HC):
                        cs, tbs_ = k // 2, k % 2
                        q.wait_ge(svh.h, a_hid_mark(cs, tbs_))
                        q.dma_start(out=HIDS[cs, tbs_, :, :], in_=hid_ring[k % 4][:]
                                    ).then_inc(sdH[k % 4].h, 16)
                        sdH[k % 4].v += 16
                        assert sdH[k % 4].v == a_spill_mark(cs, tbs_)



            @block.tensor
            def _(t):
                for c in range(HC):
                    slot = c % 2
                    if app == 1:
                        t.wait_ge(sv.h, a_quant_mark(2 * c + 1))
                    else:
                        t.wait_ge(sdA[slot].h, a2_load_mark(c))
                    for mat in range(2):
                        psl = (ps_x1, ps_x2)[mat]
                        for tb in range(2):
                            bank = psl[2 * tb + (c % 2)]
                            if c >= 2:
                                if mat == 0:
                                    t.wait_ge(ss.h, a_sig_mark(c - 2, tb))
                                else:
                                    t.wait_ge(svh.h, a_hid_mark(c - 2, tb))
                            for ic in range(IC):
                                mm = t.matmul(
                                    bank[:],
                                    q12[slot][:, mat * 2048 + ic * 128:
                                              mat * 2048 + ic * 128 + 128],
                                    nT[:, hcol(ic, tb * TBS, (tb + 1) * TBS)],
                                    start=(ic == 0), stop=(ic == IC - 1))
                            st.inc(mm)
                    assert st.v == a_mm_mark(c)

            @block.scalar
            def _(s):
                for c in range(HC):
                    for tb in range(2):
                        s.wait_ge(st.h, st_b + 4 * c + tb + 1)
                        if c >= 2:
                            s.wait_ge(svh.h, a_hid_mark(c - 2, tb))
                        ss.inc(s.activation(sig[2 * tb + (c % 2)][:],
                                            ps_x1[2 * tb + (c % 2)][:], AF.Sigmoid))
                        assert ss.v == a_sig_mark(c, tb)

            @block.vector
            def _(v):
                def emit_quant(u):
                    slot, c, hh = u % 2, u // 2, u % 2
                    if u == 0 and app == 1:
                        v.wait_ge(sv.h, thr_marks["thr12"])
                    v.wait_ge(sdA[slot].h, a_load_mark(u))
                    if u >= 4:
                        v.wait_ge(sdS[(u - 4) % 2].h, a_scr_mark(u - 4))
                    if c >= 2 and hh == 0:
                        v.wait_ge(st.h, a_mm_mark(c - 2))
                    for mat in range(2):
                        sv.inc(v.tensor_single_scalar(
                            maskr[(2 * u + mat) % 4][:],
                            w12raw[slot][:, mat * 1024:(mat + 1) * 1024],
                            thr_sb[:, 3 + mat:4 + mat], op=ALU.is_lt))
                    v.wait_ge(sv.h, sv.v)
                    for mat in range(2):
                        sv.inc(v.scalar_tensor_tensor(
                            q12[c % 2][:, mat * 2048 + hh * 1024:
                                       mat * 2048 + (hh + 1) * 1024],
                            w12raw[slot][:, mat * 1024:(mat + 1) * 1024],
                            thr_sb[:, mat:mat + 1], maskr[(2 * u + mat) % 4][:],
                            op0=ALU.is_gt, op1=ALU.subtract))
                    assert sv.v == a_quant_mark(u)

                def emit_hid(c, tb):
                    k = 2 * c + tb
                    v.wait_ge(ss.h, a_sig_mark(c, tb))
                    v.wait_ge(st.h, st_b + 4 * c + 2 + tb + 1)
                    if k >= 4:
                        v.wait_ge(sdH[k % 4].h,
                                  a_spill_mark((k - 4) // 2, (k - 4) % 2))
                    svh.inc(v.scalar_tensor_tensor(
                        hid_ring[k % 4][:], ps_x2[2 * tb + (c % 2)][:], 1.0,
                        sig[2 * tb + (c % 2)][:], op0=ALU.mult, op1=ALU.mult))
                    assert svh.v == a_hid_mark(c, tb)

                for c in range(HC):
                    if app == 1:
                        emit_quant(2 * c)
                        emit_quant(2 * c + 1)
                        if c == 0:
                            v.wait_ge(sv.h, thr_marks["thr3"])
                        v.wait_ge(sdB[c % 2].h, a_w3load_mark(c))
                        if c >= 4:
                            v.wait_ge(sdT[(c - 4) % 2].h, a_w3scr_mark(c - 4))
                        sv.inc(v.tensor_single_scalar(
                            mask3[c % 2][:], w3A[c % 2][:],
                            thr_sb[:, 5:6], op=ALU.is_lt))
                        v.wait_ge(sv.h, sv.v)
                        sv.inc(v.scalar_tensor_tensor(
                            q3A[c % 2][:], w3A[c % 2][:], thr_sb[:, 2:3],
                            mask3[c % 2][:], op0=ALU.is_gt, op1=ALU.subtract))
                        assert sv.v == a_w3q_mark(c)
                    if c >= 2:
                        emit_hid(c - 2, 0)
                        emit_hid(c - 2, 1)
                for c in (HC - 2, HC - 1):
                    emit_hid(c, 0)
                    emit_hid(c, 1)

            esa.close()
            q3 = [sb(f"q3_{app}{i}", [128, 1024], BF16, esl) for i in range(4)]
            hidl = [sb(f"hidl{app}{i}", [128, TBS], BF16, esl) for i in range(4)]
            esb = contextlib.ExitStack()

            sdB_b = [sdB[i].v for i in range(4)]
            sdR_b = [sdR[i].v for i in range(4)]
            sv3_b, st3_b = sv.v, st.v
            NSW = 4

            def b_load_mark(i):
                return sdB_b[i % 4] + 16 * (i // 4 + 1)

            def b_hid_mark(i):
                return sdR_b[i % 4] + 16 * (i // 4 + 1)

            def b_mm_mark(i):
                return st3_b + i + 1

            def b_evac_mark(swp):
                return sv3_b + 8 * (swp + 1)

            def b_evac_bank_mark(swp, m8):
                return sv3_b + 8 * swp + m8 + 1

            @block.sync
            def _(q):
                for swp in range(NSW):
                    tb, s_ = swp // 2, swp % 2
                    for c in range(HC):
                        i = swp * HC + c
                        slot = i % 4
                        if i >= 4:
                            q.wait_ge(st.h, b_mm_mark(i - 4))
                        if app == 1 and swp == 0:
                            q.wait_ge(sdT[c % 2].h, a_w3scr_mark(c))
                        q.dma_start(out=q3[slot][:],
                                    in_=Q3S[c, :, s_ * 1024:(s_ + 1) * 1024]
                                    ).then_inc(sdB[slot].h, 16)
                        sdB[slot].v += 16
                        assert sdB[slot].v == b_load_mark(i)

            @block.gpsimd
            def _(g):
                for swp in range(NSW):
                    tb, s_ = swp // 2, swp % 2
                    for c in range(HC):
                        i = swp * HC + c
                        if i >= 4:
                            g.wait_ge(st.h, b_mm_mark(i - 4))
                        g.wait_ge(sdH[(2 * c + tb) % 4].h, a_spill_mark(c, tb))
                        g.dma_start(out=hidl[i % 4][:], in_=HIDS[c, tb, :, :]
                                    ).then_inc(sdR[i % 4].h, 16)
                        sdR[i % 4].v += 16
                        assert sdR[i % 4].v == b_hid_mark(i)

            ps_o = [psum(f"pso{app}{i}", [128, TBS], F32, esb) for i in range(8)]

            @block.vector
            def _(v):
                for swp in range(NSW):
                    tb, s_ = swp // 2, swp % 2
                    v.wait_ge(st.h, b_mm_mark(swp * HC + HC - 1))
                    for m8 in range(8):
                        m = s_ * 8 + m8
                        sv.inc(v.scalar_tensor_tensor(
                            hT[:, hcol(m, tb * TBS, (tb + 1) * TBS)],
                            ps_o[m8][:], 0.5,
                            hT[:, hcol(m, tb * TBS, (tb + 1) * TBS)],
                            op0=ALU.mult, op1=ALU.add))
                    assert sv.v == b_evac_mark(swp)

            @block.tensor
            def _(t):
                for swp in range(NSW):
                    tb, s_ = swp // 2, swp % 2
                    for c in range(HC):
                        i = swp * HC + c
                        slot = i % 4
                        t.wait_ge(sdB[slot].h, b_load_mark(i))
                        t.wait_ge(sdR[i % 4].h, b_hid_mark(i))
                        if c == 0 and swp == 0:
                            t.wait_ge(ss.h, a_sig_mark(HC - 1, 1))
                            t.wait_ge(svh.h, a_hid_mark(HC - 1, 1))
                        for m8 in range(8):
                            if c == 0 and swp > 0:
                                t.wait_ge(sv.h, b_evac_bank_mark(swp - 1, m8))
                            mm = t.matmul(ps_o[m8][:],
                                          q3[slot][:, m8 * 128:(m8 + 1) * 128],
                                          hidl[i % 4][:],
                                          start=(c == 0), stop=(c == HC - 1))
                        st.inc(mm)
                        assert st.v == b_mm_mark(i)

            barrier()
            esb.close()
            esl.close()

        def emit_attn():
            esl = contextlib.ExitStack()
            kT = sb("kT", [128, NKV * 2048], BF16, esl)
            v_tm = sb("v_tm", [128, NKV * 2048], BF16, esl)
            wring = [sb(f"wring{i}", [128, 2048], BF16, esl) for i in range(2)]
            den_b = sb("den_b", [1, TBS], BF16, esl)
            den_f = sb("den_f", [1, TBS], F32, esl)
            negc_sb = sb("negc_sb", [128, 1], F32, esl)
            esvt = contextlib.ExitStack()
            vT = sb("vT", [128, NKV * 2048], BF16, esvt)
            sqa = sb("sqa", [128, 2048], BF16, esvt)

            ps_qp = psum("ps_qp", [128, T], F32, esl)

            sgq_b = [sgq[i].v for i in range(2)]
            st_a, ss_a, sv_a, sg_a = st.v, ss.v, sv.v, sg.v

            ring_src = [(WKL, 0), (WKL, 1), (WVL, 0), (WVL, 1)] \
                + [(WQL, i) for i in range(NH)] + [(WOL, i) for i in range(DC)]

            def ring_mark(i):
                return sgq_b[i % 2] + 16 * (i // 2 + 1)

            A0 = st_a
            order = []
            for jp in range(8):
                order.append(("sc", jp))
                if jp >= 1:
                    order.append(("pv", jp - 1))
            order.append(("pv", 7))
            sc_off, pv_off = [0] * 8, [0] * 8
            for idx_, (kind, jp) in enumerate(order):
                (sc_off if kind == "sc" else pv_off)[jp] = idx_ + 1

            def kv_mm_mark(oc, tb=1):
                return A0 + 2 * oc + tb + 1

            def vt_mark(grp):
                return A0 + 8 + grp + 1

            def kn_mm_mark(g_, k_):
                return A0 + 16 + 4 * g_ + k_ + 1

            QP0 = A0 + 24

            def qproj_mark(hh, tb=1):
                return QP0 + 4 * hh + tb + 1

            def qnA_mark(hh):
                return QP0 + 4 * hh + 3

            def qnB_mark(hh):
                return QP0 + 4 * hh + 4

            negc_bc_mark = QP0 + 65
            H0 = QP0 + 65

            def hb(idx):
                return H0 + 17 * idx

            def sc_mark(idx, jp):
                return hb(idx) + sc_off[jp]

            def pv_mark(idx, jp):
                return hb(idx) + pv_off[jp]

            def dbc_mark(idx):
                return hb(idx) + 17

            OP0 = H0 + 17 * 32

            def oproj_mark(m, tb):
                return OP0 + 2 * m + tb + 1

            SS0 = ss_a + 8

            def ss_qh_mark(hh, tb):
                return SS0 + 2 * hh + tb + 1

            ss_negc_mark = SS0 + 34

            def exp_mark(idx, jp):
                return SS0 + 34 + 10 * idx + jp + 1

            def ss_recip_mark(idx):
                return SS0 + 34 + 10 * idx + 10

            SV0 = sv_a + 8 + 13

            def qn_base(hh):
                return SV0 + 5 * hh

            NC0 = SV0 + 80
            sv_negc_mark = NC0 + 4
            HD0 = NC0 + 4

            def sv_rrep_mark(idx):
                return HD0 + 2 * idx + 1

            def sv_ao_mark(idx):
                return HD0 + 2 * idx + 2

            OE0 = HD0 + 64

            def sv_oevac_mark(m, tb):
                return OE0 + 2 * m + tb + 1

            def emit_ring(g, lo, hi):
                for i in range(lo, hi):
                    W, idx = ring_src[i]
                    slot = i % 2
                    if i >= 2:
                        j = i - 2
                        if j < 4:
                            g.wait_ge(st.h, kv_mm_mark(j))
                        elif j < 4 + NH:
                            g.wait_ge(st.h, qproj_mark(j - 4))
                        else:
                            g.wait_ge(st.h, oproj_mark(j - 4 - NH, 1))
                    g.dma_start(out=wring[slot][:], in_=W[idx, :, :]
                                ).then_inc(sgq[slot].h, 16)
                    sgq[slot].v += 16
                    assert sgq[slot].v == ring_mark(i)

            @block.gpsimd
            def _(g):
                emit_ring(g, 0, 4)

            @block.tensor
            def _(t):
                for oc in range(4):
                    slot = oc % 2
                    t.wait_ge(sgq[slot].h, ring_mark(oc))
                    for tb in range(2):
                        if oc >= 2:
                            t.wait_ge(ss.h, ss_a + 2 * (oc - 2) + tb + 1)
                        for ic in range(IC):
                            mm = t.matmul(ps_qp[:, tb * TBS:(tb + 1) * TBS],
                                          wring[slot][:, ic * 128:(ic + 1) * 128],
                                          nT[:, hcol(ic, tb * TBS, (tb + 1) * TBS)],
                                          start=(ic == 0), stop=(ic == IC - 1))
                        st.inc(mm)
                    assert st.v == kv_mm_mark(oc)

            @block.scalar
            def _(s):
                for oc in range(4):
                    g_, isv = oc % 2, oc // 2
                    dstT = vT if isv else kT
                    bias = (bv_sb if isv else bk_sb)[:, g_:g_ + 1]
                    for tb in range(2):
                        s.wait_ge(st.h, A0 + 2 * oc + tb + 1)
                        ss.inc(s.activation(
                            dstT[:, g_ * 2048 + tb * TBS:g_ * 2048 + (tb + 1) * TBS],
                            ps_qp[:, tb * TBS:(tb + 1) * TBS],
                            AF.Identity, bias=bias))
            assert ss.v == ss_a + 8

            @block.gpsimd
            def _(g):
                g.wait_ge(sd_b.h, 96)
                g.reg_load(rh, half_sb[:1, :1])
                g.reg_load(rn, nonce_sb[:1, :1])
                g.reg_mov(rit, 0)
                g.wait_ge(ss.h, ss_a + 8)
                for r in range(4):
                    isv, g_ = r // 2, r % 2
                    src = (vT if isv else kT)[:, g_ * 2048:g_ * 2048 + T]
                    with g.If_eq(rh, 0):
                        g.dma_start(out=XKV[0, r, :, :], in_=src).then_inc(sgkv.h, 16)
                    with g.Else():
                        g.dma_start(out=XKV[1, r, :, :], in_=src).then_inc(sgkv.h, 16)
                    sgkv.v += 16
                g.wait_ge(sgkv.h, sgkv.v)
                with g.If_eq(rh, 0):
                    g.dma_start(out=XKF[0:1, :], in_=nonce_sb[:]).then_inc(sgfl.h, 16)
                with g.Else():
                    g.dma_start(out=XKF[1:2, :], in_=nonce_sb[:]).then_inc(sgfl.h, 16)
                sgfl.v += 16
                g.wait_ge(sgfl.h, sgfl.v)
                g.reg_mov(rp, 1)
                with g.While(rp):
                    with g.If_eq(rh, 0):
                        g.dma_start(out=flag_sb[:], in_=XKF[1:2, :]).then_inc(sgpl.h, 16)
                    with g.Else():
                        g.dma_start(out=flag_sb[:], in_=XKF[0:1, :]).then_inc(sgpl.h, 16)
                    g.reg_add(rit, rit, 16)
                    g.wait_ge(sgpl.h, rit)
                    g.reg_load(rf, flag_sb[:1, :1])
                    g.reg_sub(rp, rf, rn)
                for r in range(4):
                    isv, g_ = r // 2, r % 2
                    dst = (vT if isv else kT)[:, g_ * 2048 + T:(g_ + 1) * 2048]
                    with g.If_eq(rh, 0):
                        g.dma_start(out=dst, in_=XKV[1, r, :, :]).then_inc(sgin.h, 16)
                    with g.Else():
                        g.dma_start(out=dst, in_=XKV[0, r, :, :]).then_inc(sgin.h, 16)
                    sgin.v += 16
            sgin_kv = sgin.v
            sgkv_out = sgkv.v

            @block.gpsimd
            def _(g):
                emit_ring(g, 4, len(ring_src))

            estr = contextlib.ExitStack()
            ps_tr = [psum(f"ps_tr{i}", [128, 512], BF16, estr) for i in range(2)]

            @block.tensor
            def _(t):
                t.wait_ge(sgin.h, sgin_kv)
                t.wait_ge(sgb.h, 16)
                for grp in range(8):
                    g_, jq = grp // 4, grp % 4
                    if grp >= 2:
                        t.wait_ge(sv.h, sv_a + grp - 1)
                    for jj in range(4):
                        j = 4 * jq + jj
                        tr = t.transpose(ps_tr[grp % 2][:, jj * 128:(jj + 1) * 128],
                                         vT[:, g_ * 2048 + j * 128:
                                            g_ * 2048 + (j + 1) * 128],
                                         idb[:])
                    st.inc(tr)
                    assert st.v == vt_mark(grp)

            @block.vector
            def _(v):
                for grp in range(8):
                    g_, jq = grp // 4, grp % 4
                    v.wait_ge(st.h, vt_mark(grp))
                    sv.inc(v.tensor_copy(
                        v_tm[:, g_ * 2048 + jq * 512:g_ * 2048 + (jq + 1) * 512],
                        ps_tr[grp % 2][:]))
            assert sv.v == sv_a + 8

            esvt.close()
            qT = sb("qT", [128, NH * T], BF16, esl)
            sqa1k = sb("sqa1k", [128, T], BF16, esl)
            aoT = sb("aoT", [128, DC * T], BF16, esl)
            probs = [sb(f"probs{i}", [128, T], BF16, esl) for i in range(3)]

            estr.close()
            ps_sc = [psum(f"ps_sc{i}", [128, T], F32, esl) for i in range(2)]
            ps_po = [psum(f"ps_po{i}", [128, TBS], F32, esl) for i in range(2)]

            def kn_sq_mark(g_):
                return sv_a + 8 + 6 * g_ + 1

            def kn_col_mark(g_, k_):
                return sv_a + 8 + 6 * g_ + 1 + k_ + 1

            @block.vector
            def _(v):
                for g_ in range(2):
                    v.wait_ge(sgin.h, sgin_kv)
                    if g_ == 1:
                        v.wait_ge(st.h, kn_mm_mark(0, 3))
                    sv.inc(v.tensor_mul(sqa[:], kT[:, g_ * 2048:(g_ + 1) * 2048],
                                        kT[:, g_ * 2048:(g_ + 1) * 2048]))
                    assert sv.v == kn_sq_mark(g_)
                    for k_ in range(4):
                        v.wait_ge(st.h, kn_mm_mark(g_, k_))
                        sv.inc(v.reduce_max(rows_sb[:, 44 + k_:45 + k_],
                                            ps_po[1][0:1, 0:TBS], axis=AX.X))
                        assert sv.v == kn_col_mark(g_, k_)
                    v.wait_ge(sv.h, sv.v)
                    sv.inc(v.reduce_max(kmax2_sb[:, g_:g_ + 1], rows_sb[:, 44:48],
                                        axis=AX.X))
                sv.inc(v.memset(rows_sb[:, 40:41], 0.0))
                assert sv.v == SV0

            @block.tensor
            def _(t):
                for g_ in range(2):
                    t.wait_ge(sv.h, kn_sq_mark(g_))
                    for k_ in range(4):
                        if (g_, k_) != (0, 0):
                            pk = (g_, k_ - 1) if k_ else (g_ - 1, 3)
                            t.wait_ge(sv.h, kn_col_mark(*pk))
                        st.inc(t.matmul(ps_po[1][0:1, 0:TBS], ones_col_b[:],
                                        sqa[:, k_ * TBS:(k_ + 1) * TBS],
                                        start=True, stop=True))
                        assert st.v == kn_mm_mark(g_, k_)


            @block.tensor
            def _(t):
                for hh in range(NH):
                    slot = (4 + hh) % 2
                    t.wait_ge(sgq[slot].h, ring_mark(4 + hh))
                    for tb in range(2):
                        if hh >= 1:
                            t.wait_ge(ss.h, ss_qh_mark(hh - 1, tb))
                        else:
                            t.wait_ge(ss.h, ss_a + 8)
                        for ic in range(IC):
                            mm = t.matmul(ps_qp[:, tb * TBS:(tb + 1) * TBS],
                                          wring[slot][:, ic * 128:(ic + 1) * 128],
                                          nT[:, hcol(ic, tb * TBS, (tb + 1) * TBS)],
                                          start=(ic == 0), stop=(ic == IC - 1))
                        st.inc(mm)
                        assert st.v == qproj_mark(hh, tb)
                    t.wait_ge(sv.h, qn_base(hh) + 1)
                    st.inc(t.matmul(ps_po[1][0:1, 0:TBS], ones_col_b[:],
                                    sqa1k[:, 0:TBS], start=True, stop=True))
                    t.wait_ge(sv.h, qn_base(hh) + 2)
                    st.inc(t.matmul(ps_po[1][0:1, 0:TBS], ones_col_b[:],
                                    sqa1k[:, TBS:2 * TBS], start=True, stop=True))
                    assert st.v == qnB_mark(hh)
                t.wait_ge(ss.h, ss_negc_mark)
                st.inc(t.matmul(ps_po[0][:, 0:1], negones_row_f[:],
                                rows_sb[:, 36:37], start=True, stop=True))
                assert st.v == negc_bc_mark

            @block.scalar
            def _(s):
                for hh in range(NH):
                    for tb in range(2):
                        s.wait_ge(st.h, qproj_mark(hh, tb))
                        if hh == 0 and tb == 0:
                            s.wait_ge(st.h, vt_mark(7))
                            s.wait_ge(sgkv.h, sgkv_out)
                        ss.inc(s.activation(qT[:, hh * T + tb * TBS:
                                               hh * T + (tb + 1) * TBS],
                                            ps_qp[:, tb * TBS:(tb + 1) * TBS],
                                            AF.Identity, bias=bq_sb[:, hh:hh + 1]))
                        assert ss.v == ss_qh_mark(hh, tb)
                s.wait_ge(sv.h, NC0 + 3)
                ss.inc(s.activation(rows_sb[:, 36:37], rows_sb[:, 38:39], AF.Ln))
                s.wait_ge(ss.h, ss.v)
                ss.inc(s.activation(rows_sb[:, 36:37], rows_sb[:, 36:37],
                                    AF.Exp, scale=0.5))
                assert ss.v == ss_negc_mark
                for idx in range(32):
                    hh, tb = idx // 2, idx % 2
                    for jp in range(8):
                        par = jp % 2
                        s.wait_ge(st.h, sc_mark(idx, jp))
                        if idx == 0 and jp == 0:
                            s.wait_ge(sv.h, sv_negc_mark)
                        gj = 8 * idx + jp
                        if gj >= 3:
                            pidx, pjp = divmod(gj - 3, 8)
                            s.wait_ge(st.h, pv_mark(pidx, pjp))
                        ss.inc(s.activation(probs[gj % 3][:], ps_sc[par][:],
                                            AF.Exp, bias=negc_sb[:, 0:1],
                                            scale=SCALE))
                        assert ss.v == exp_mark(idx, jp)
                    s.wait_ge(st.h, pv_mark(idx, 7))
                    ss.inc(s.activation(den_f[:], ps_qp[0:1, TBS:2 * TBS], AF.Ln))
                    s.wait_ge(ss.h, ss.v)
                    ss.inc(s.activation(den_b[:], den_f[:], AF.Exp, scale=-1.0))
                    assert ss.v == ss_recip_mark(idx)

            @block.vector
            def _(v):
                for hh in range(NH):
                    v.wait_ge(ss.h, ss_qh_mark(hh, 1))
                    if hh >= 1:
                        v.wait_ge(st.h, qnB_mark(hh - 1))
                    sv.inc(v.tensor_mul(sqa1k[:], qT[:, hh * T:(hh + 1) * T],
                                        qT[:, hh * T:(hh + 1) * T]))
                    assert sv.v == qn_base(hh) + 1
                    v.wait_ge(st.h, qnA_mark(hh))
                    sv.inc(v.reduce_max(rows_sb[:, 32:33], ps_po[1][0:1, 0:TBS],
                                        axis=AX.X))
                    v.wait_ge(st.h, qnB_mark(hh))
                    sv.inc(v.reduce_max(rows_sb[:, 33:34], ps_po[1][0:1, 0:TBS],
                                        axis=AX.X))
                    v.wait_ge(sv.h, sv.v)
                    sv.inc(v.tensor_max(rows_sb[:, 40:41], rows_sb[:, 40:41],
                                        rows_sb[:, 32:33]))
                    v.wait_ge(sv.h, sv.v)
                    sv.inc(v.tensor_max(rows_sb[:, 40:41], rows_sb[:, 40:41],
                                        rows_sb[:, 33:34]))
                    assert sv.v == qn_base(hh) + 5
                sv.inc(v.reduce_max(rows_sb[:, 41:42], kmax2_sb[:, 0:2], axis=AX.X))
                v.wait_ge(sv.h, sv.v)
                sv.inc(v.tensor_mul(rows_sb[:, 42:43], rows_sb[:, 40:41],
                                    rows_sb[:, 41:42]))
                v.wait_ge(sv.h, sv.v)
                sv.inc(v.tensor_single_scalar(rows_sb[:, 38:39], rows_sb[:, 42:43],
                                              1.0 / HD, op=ALU.mult))
                assert sv.v == NC0 + 3
                v.wait_ge(st.h, negc_bc_mark)
                sv.inc(v.tensor_copy(negc_sb[:], ps_po[0][:, 0:1]))
                assert sv.v == sv_negc_mark
                for idx in range(32):
                    hh, tb = idx // 2, idx % 2
                    v.wait_ge(st.h, dbc_mark(idx))
                    sv.inc(v.tensor_copy(rrep[:, 0:TBS], ps_sc[0][:, 0:TBS]))
                    assert sv.v == sv_rrep_mark(idx)
                    v.wait_ge(sv.h, sv.v)
                    sv.inc(v.tensor_mul(
                        aoT[:, hcol(hh, tb * TBS, (tb + 1) * TBS)],
                        ps_po[idx % 2][:], rrep[:, 0:TBS]))
                    assert sv.v == sv_ao_mark(idx)
                for m in range(DC):
                    for tb in range(2):
                        v.wait_ge(st.h, oproj_mark(m, tb))
                        sv.inc(v.scalar_tensor_tensor(
                            hT[:, hcol(m, tb * TBS, (tb + 1) * TBS)],
                            ps_qp[:, tb * TBS:(tb + 1) * TBS],
                            bo_sb[:, m:m + 1],
                            hT[:, hcol(m, tb * TBS, (tb + 1) * TBS)],
                            op0=ALU.add, op1=ALU.add))
                        assert sv.v == sv_oevac_mark(m, tb)

            @block.tensor
            def _(t):
                for idx in range(32):
                    hh, tb = idx // 2, idx % 2
                    g_ = hh // 8
                    for kind, jp in order:
                        if kind == "sc":
                            par = jp % 2
                            if jp >= 2:
                                t.wait_ge(ss.h, exp_mark(idx, jp - 2))
                            elif idx >= 1:
                                t.wait_ge(ss.h, exp_mark(idx - 1, 6 + jp))
                            if jp == 0:
                                t.wait_ge(ss.h, ss_qh_mark(hh, tb))
                                if idx >= 1:
                                    t.wait_ge(sv.h, sv_rrep_mark(idx - 1))
                            for jj in range(2):
                                j = 2 * jp + jj
                                mm = t.matmul(
                                    ps_sc[par][:, jj * TBS:(jj + 1) * TBS],
                                    kT[:, g_ * 2048 + j * 128:
                                       g_ * 2048 + (j + 1) * 128],
                                    qT[:, hh * T + tb * TBS:hh * T + (tb + 1) * TBS],
                                    start=True, stop=True)
                            st.inc(mm)
                            assert st.v == sc_mark(idx, jp)
                        else:
                            t.wait_ge(ss.h, exp_mark(idx, jp))
                            if jp == 0:
                                if idx == 0:
                                    t.wait_ge(sv.h, sv_negc_mark)
                                if idx >= 2:
                                    t.wait_ge(sv.h, sv_ao_mark(idx - 2))
                                if idx >= 1:
                                    t.wait_ge(ss.h, ss_recip_mark(idx - 1))
                            gslot = (8 * idx + jp) % 3
                            for jj in range(2):
                                j = 2 * jp + jj
                                t.matmul(ps_po[idx % 2][:],
                                         v_tm[:, g_ * 2048 + j * 128:
                                              g_ * 2048 + (j + 1) * 128],
                                         probs[gslot][:, jj * TBS:(jj + 1) * TBS],
                                         start=(jp == 0 and jj == 0),
                                         stop=(jp == 7 and jj == 1))
                            for jj in range(2):
                                mm = t.matmul(ps_qp[0:1, TBS:2 * TBS], ones_col_b[:],
                                              probs[gslot][:, jj * TBS:
                                                           (jj + 1) * TBS],
                                              start=(jp == 0 and jj == 0),
                                              stop=(jp == 7 and jj == 1))
                            st.inc(mm)
                            assert st.v == pv_mark(idx, jp)
                    t.wait_ge(ss.h, ss_recip_mark(idx))
                    st.inc(t.matmul(ps_sc[0][:, 0:TBS], ones_row_b[:],
                                    den_b[:], start=True, stop=True))
                    assert st.v == dbc_mark(idx)
                for m in range(DC):
                    slot = (4 + NH + m) % 2
                    t.wait_ge(sgq[slot].h, ring_mark(4 + NH + m))
                    if m == 0:
                        t.wait_ge(sv.h, sv_ao_mark(31))
                        t.wait_ge(ss.h, ss_qh_mark(NH - 1, 1))
                    for tb in range(2):
                        if m >= 1:
                            t.wait_ge(sv.h, sv_oevac_mark(m - 1, tb))
                        for ac in range(IC):
                            mm = t.matmul(ps_qp[:, tb * TBS:(tb + 1) * TBS],
                                          wring[slot][:, ac * 128:(ac + 1) * 128],
                                          aoT[:, hcol(ac, tb * TBS, (tb + 1) * TBS)],
                                          start=(ac == 0), stop=(ac == IC - 1))
                        st.inc(mm)
                        assert st.v == oproj_mark(m, tb)

            barrier()
            esl.close()

        emit_ln("1")
        emit_gamma3_thr()
        esg.close()
        emit_ffn(1)
        emit_ln("2")
        emit_attn()
        emit_ln("3")
        emit_ffn(2)

        @block.sync
        def _(q):
            for c in range(DC):
                q.dma_start(out=OUT[c * 128:(c + 1) * 128, :], in_=hT[:, hcol(c)]
                            ).then_inc(sd_out.h, 16)
            sd_out.v += 16 * DC
            q.wait_ge(sd_out.h, sd_out.v)

    es.close()
    return nc



def _layout_w(w):
    O, I = w.shape
    return np.ascontiguousarray(
        w.reshape(O // 128, 128, I // 128, 128).transpose(0, 3, 2, 1)
    ).reshape(O // 128, 128, I)


def kernel(L0_attn_x, L0_attn_w_q, L0_attn_b_q, L0_attn_w_k, L0_attn_b_k,
           L0_attn_w_v, L0_attn_b_v, L0_attn_w_o, L0_attn_b_o,
           L0_ffn_w1, L0_ffn_w2, L0_ffn_w3):
    x = np.asarray(L0_attn_x, dtype=np.float32)
    W1 = _layout_w(np.asarray(L0_ffn_w1, np.float32))
    W2 = _layout_w(np.asarray(L0_ffn_w2, np.float32))
    W3 = np.ascontiguousarray(np.asarray(L0_ffn_w3, np.float32).T
                              ).reshape(HC, 128, 2048)
    WQ = _layout_w(np.asarray(L0_attn_w_q, np.float32))
    WK = _layout_w(np.asarray(L0_attn_w_k, np.float32))
    WV = _layout_w(np.asarray(L0_attn_w_v, np.float32))
    WO = _layout_w(np.asarray(L0_attn_w_o, np.float32))
    BQa = np.ascontiguousarray(np.asarray(L0_attn_b_q, np.float32).reshape(16, 128).T)
    BKa = np.ascontiguousarray(np.asarray(L0_attn_b_k, np.float32).reshape(2, 128).T)
    BVa = np.ascontiguousarray(np.asarray(L0_attn_b_v, np.float32).reshape(2, 128).T)
    BOa = np.ascontiguousarray(np.asarray(L0_attn_b_o, np.float32).reshape(16, 128).T)
    IDENTa = np.eye(128, dtype=np.float32)
    nonce = np.array([[np.random.randint(1, 2 ** 30)]], np.int32)

    in_maps = []
    for core in range(8):
        b, h = core // 2, core % 2
        xt = np.ascontiguousarray(x[b, h * 1024:(h + 1) * 1024, :].T)
        in_maps.append({
            "XT": xt, "W1L": W1, "W2L": W2, "W3L": W3,
            "WQL": WQ, "WKL": WK, "WVL": WV, "WOL": WO,
            "BQ": BQa, "BK": BKa, "BV": BVa, "BO": BOa,
            "IDENT": IDENTa,
            "HALF": np.array([[h]], np.int32), "NONCE": nonce,
        })
    nc = build_program()
    res = run_bass_kernel_spmd(nc, in_maps, core_ids=list(range(8)))
    out = np.empty((4, 2048, 2048), np.float32)
    for core in range(8):
        b, h = core // 2, core % 2
        out[b, h * 1024:(h + 1) * 1024, :] = np.asarray(res.results[core]["OUT"]).T
    return out


# version 12
# speedup vs baseline: 1.2907x; 1.0322x over previous
import contextlib
import numpy as np
import concourse.bass as bass
import concourse.mybir as mybir
from concourse.bass_utils import run_bass_kernel_spmd

F32 = mybir.dt.float32
BF16 = mybir.dt.bfloat16
I32 = mybir.dt.int32
AF = mybir.ActivationFunctionType
ALU = mybir.AluOpType
AX = mybir.AxisListType

NB, SEQ, D, HID = 4, 2048, 2048, 5632
T = 1024
TBS = 512
DC = D // 128
IC = 16
HC = HID // 128
NH, NKV, HD = 16, 2, 128
SCALE = 1.0 / float(np.sqrt(np.float64(HD)))
NW = HID * D
EPS_Q = 1e-8
EPS_LN = 1e-5


class Cnt:
    REG = []

    def __init__(self, es, nc, name):
        self.h = es.enter_context(nc.semaphore(name))
        self.v = 0
        self.name = name
        Cnt.REG.append(self)

    def inc(self, inst, k=1):
        inst.then_inc(self.h, k)
        self.v += k
        return self.v


def build_program():
    Cnt.REG = []
    nc = bass.Bass("TRN2", target_bir_lowering=False,
                   detect_race_conditions=False)

    XT = nc.dram_tensor("XT", [D, T], F32, kind="ExternalInput").ap()
    W1L = nc.dram_tensor("W1L", [HC, 128, 2048], F32, kind="ExternalInput").ap()
    W2L = nc.dram_tensor("W2L", [HC, 128, 2048], F32, kind="ExternalInput").ap()
    W3L = nc.dram_tensor("W3L", [HC, 128, 2048], F32, kind="ExternalInput").ap()
    WQL = nc.dram_tensor("WQL", [NH, 128, 2048], F32, kind="ExternalInput").ap()
    WKL = nc.dram_tensor("WKL", [NKV, 128, 2048], F32, kind="ExternalInput").ap()
    WVL = nc.dram_tensor("WVL", [NKV, 128, 2048], F32, kind="ExternalInput").ap()
    WOL = nc.dram_tensor("WOL", [DC, 128, 2048], F32, kind="ExternalInput").ap()
    BQ = nc.dram_tensor("BQ", [128, NH], F32, kind="ExternalInput").ap()
    BK = nc.dram_tensor("BK", [128, NKV], F32, kind="ExternalInput").ap()
    BV = nc.dram_tensor("BV", [128, NKV], F32, kind="ExternalInput").ap()
    BO = nc.dram_tensor("BO", [128, DC], F32, kind="ExternalInput").ap()
    IDENT = nc.dram_tensor("IDENT", [128, 128], F32, kind="ExternalInput").ap()
    HALF = nc.dram_tensor("HALF", [1, 1], I32, kind="ExternalInput").ap()
    NONCE = nc.dram_tensor("NONCE", [1, 1], I32, kind="ExternalInput").ap()
    OUT = nc.dram_tensor("OUT", [D, T], F32, kind="ExternalOutput").ap()

    Q12S = nc.dram_tensor("Q12S", [HC, 2, 128, 2048], BF16, kind="Internal").ap()
    Q3S = nc.dram_tensor("Q3S", [HC, 128, 2048], BF16, kind="Internal").ap()
    HIDS = nc.dram_tensor("HIDS", [HC, 2, 128, TBS], BF16, kind="Internal").ap()
    XKV = nc.dram_tensor("XKV", [2, 4, 128, T], BF16, kind="Internal",
                         addr_space="Shared").ap()
    XG = nc.dram_tensor("XG", [2, 4], F32, kind="Internal", addr_space="Shared").ap()
    XGF = nc.dram_tensor("XGF", [2, 16], I32, kind="Internal", addr_space="Shared").ap()
    XKF = nc.dram_tensor("XKF", [2, 16], I32, kind="Internal", addr_space="Shared").ap()

    es = contextlib.ExitStack()

    def sb(name, shape, dtype, st_=None):
        return (st_ or es).enter_context(nc.sbuf_tensor(name, shape, dtype))

    def psum(name, shape, dtype=F32, st_=None):
        return (st_ or es).enter_context(nc.psum_tensor(name, shape, dtype))

    hT = sb("hT", [128, DC * T], F32)
    nT = sb("nT", [128, DC * T], BF16)
    rrep = sb("rrep", [128, T], F32)
    ones_col_f = sb("ones_col_f", [128, 1], F32)
    ones_col_b = sb("ones_col_b", [128, 1], BF16)
    ones_row_f = sb("ones_row_f", [1, 128], F32)
    ones_row_b = sb("ones_row_b", [1, 128], BF16)
    negones_row_f = sb("negones_row_f", [1, 128], F32)
    bq_sb = sb("bq_sb", [128, NH], F32)
    bk_sb = sb("bk_sb", [128, NKV], F32)
    bv_sb = sb("bv_sb", [128, NKV], F32)
    bo_sb = sb("bo_sb", [128, DC], F32)
    idb = sb("idb", [128, 128], BF16)
    half_sb = sb("half_sb", [1, 1], I32)
    nonce_sb = sb("nonce_sb", [1, 16], I32)
    flag_sb = sb("flag_sb", [1, 16], I32)
    thr_sb = sb("thr_sb", [128, 8], F32)
    rows_sb = sb("rows_sb", [1, 64], F32)
    g3row = sb("g3row", [1, 4], F32)
    growB = sb("growB", [1, 4], F32)
    kmax2_sb = sb("kmax2_sb", [1, 2], F32)
    bump_sb = sb("bump_sb", [1, 8], F32)

    rh = es.enter_context(nc.gpsimd.register("rh"))
    rn = es.enter_context(nc.gpsimd.register("rn"))
    rf = es.enter_context(nc.gpsimd.register("rf"))
    rp = es.enter_context(nc.gpsimd.register("rp"))
    rit = es.enter_context(nc.gpsimd.register("rit"))
    qrh = es.enter_context(nc.sync.register("qrh"))

    sv = Cnt(es, nc, "sv")
    svh = Cnt(es, nc, "svh")
    ss = Cnt(es, nc, "ss")
    st = Cnt(es, nc, "st")
    sg = Cnt(es, nc, "sg")
    sd_x = Cnt(es, nc, "sd_x")
    sd_b = Cnt(es, nc, "sd_b")
    sd_out = Cnt(es, nc, "sd_out")
    sdG = [Cnt(es, nc, f"sdG{i}") for i in range(4)]
    sdA = [Cnt(es, nc, f"sdA{i}") for i in range(2)]
    sdB = [Cnt(es, nc, f"sdB{i}") for i in range(4)]
    sdS = [Cnt(es, nc, f"sdS{i}") for i in range(2)]
    sdT = [Cnt(es, nc, f"sdT{i}") for i in range(2)]
    sdH = [Cnt(es, nc, f"sdH{i}") for i in range(4)]
    sdR = [Cnt(es, nc, f"sdR{i}") for i in range(4)]
    sgq = [Cnt(es, nc, f"sgq{i}") for i in range(2)]
    sgkv = Cnt(es, nc, "sgkv")
    sgfl = Cnt(es, nc, "sgfl")
    sgpl = Cnt(es, nc, "sgpl")
    sgin = Cnt(es, nc, "sgin")
    sgb = Cnt(es, nc, "sgb")

    thr_marks = {}

    def hcol(c, t0=0, t1=T):
        return slice(c * T + t0, c * T + t1)

    with nc.Block() as block:

        def barrier():
            snap = [(c, c.v) for c in Cnt.REG if c.v > 0]

            def waits(q, own):
                for c, v in snap:
                    if c is not own:
                        q.wait_ge(c.h, v)

            @block.sync
            def _(q):
                waits(q, None)

            @block.vector
            def _(v):
                waits(v, sv)

            @block.scalar
            def _(s):
                waits(s, ss)

            @block.tensor
            def _(t):
                waits(t, st)

            @block.gpsimd
            def _(g):
                waits(g, sg)

        @block.sync
        def _(q):
            for c in range(DC):
                q.dma_start(out=hT[:, hcol(c)], in_=XT[c * 128:(c + 1) * 128, :]
                            ).then_inc(sd_x.h, 16)
            sd_x.v += 16 * DC
            for dst, src in ((bq_sb, BQ), (bk_sb, BK), (bv_sb, BV), (bo_sb, BO)):
                q.dma_start(out=dst[:], in_=src).then_inc(sd_b.h, 16)
            q.dma_start(out=half_sb[:], in_=HALF).then_inc(sd_b.h, 16)
            q.dma_start(out=nonce_sb[:, 0:1], in_=NONCE).then_inc(sd_b.h, 16)
            sd_b.v += 16 * 6

        @block.gpsimd
        def _(g):
            g.dma_start(out=idb[:], in_=IDENT).then_inc(sgb.h, 16)
            sgb.v += 16

        @block.vector
        def _(v):
            v.memset(ones_col_f[:], 1.0)
            v.memset(ones_col_b[:], 1.0)
            v.memset(ones_row_f[:], 1.0)
            v.memset(ones_row_b[:], 1.0)
            v.memset(negones_row_f[:], -1.0)
            v.memset(g3row[:], 0.0)
            v.memset(growB[:], 0.0)
            v.memset(nonce_sb[:, 1:16], 0)
            sv.inc(v.memset(bump_sb[:], 0.0))

        esg = contextlib.ExitStack()
        gsl = [sb(f"gsl{i}", [128, 2048], F32, esg) for i in range(4)]
        gpart = sb("gpart", [128, 144], F32, esg)
        gdump = sb("gdump", [128, 2048], F32, esg)
        ps_g3 = psum("ps_g3", [1, 4], F32, esg)
        ps_thr = psum("ps_thr", [128, 8], F32, esg)
        NG = 3 * HC

        sv_g_base = sv.v
        ss_g_base = ss.v

        def g_red_mark(i):
            if i >= 2 * HC:
                return ss_g_base + HC + (i - 2 * HC) + 1
            if i % 2 == 0:
                return sv_g_base + i // 2 + 1
            return ss_g_base + i // 2 + 1

        @block.gpsimd
        def _(g):
            for i in range(NG):
                slot = i % 4
                mat, j = i // HC, i % HC
                W = (W1L, W2L, W3L)[mat]
                if i >= 4:
                    ip = i - 4
                    if ip < 2 * HC and ip % 2 == 0:
                        g.wait_ge(sv.h, g_red_mark(ip))
                    else:
                        g.wait_ge(ss.h, g_red_mark(ip))
                g.dma_start(out=gsl[slot][:], in_=W[j, :, :]
                            ).then_inc(sdG[slot].h, 16)
                sdG[slot].v += 16

        @block.vector
        def _(v):
            for i in range(0, 2 * HC, 2):
                v.wait_ge(sdG[i % 4].h, 16 * (i // 4 + 1))
                sv.inc(v.reduce_sum(gpart[:, i:i + 1], gsl[i % 4][:], axis=AX.X,
                                    apply_absolute_value=True))
                assert sv.v == g_red_mark(i)
            v.wait_ge(ss.h, g_red_mark(2 * HC - 1))
            for m in range(2):
                v.wait_ge(sv.h, sv.v)
                sv.inc(v.reduce_sum(gpart[:, 136 + m:137 + m],
                                    gpart[:, m * HC:(m + 1) * HC], axis=AX.X))
        sv_gpart12 = sv.v

        @block.scalar
        def _(s):
            for i in range(1, 2 * HC, 2):
                s.wait_ge(sdG[i % 4].h, 16 * (i // 4 + 1))
                ss.inc(s.activation(gdump[:], gsl[i % 4][:], AF.Abs,
                                    accum_out=gpart[:, i:i + 1]))
                assert ss.v == g_red_mark(i)

        @block.tensor
        def _(t):
            t.wait_ge(sv.h, sv_gpart12)
            for m in range(2):
                mm = t.matmul(ps_g3[:, m:m + 1], gpart[:, 136 + m:137 + m],
                              ones_col_f[:], start=True, stop=True)
            st.inc(mm)
        st_g12 = st.v

        @block.vector
        def _(v):
            v.wait_ge(st.h, st_g12)
            sv.inc(v.tensor_copy(g3row[:, 0:2], ps_g3[:, 0:2]))
            v.wait_ge(sv.h, sv.v)
            v.tensor_scalar(rows_sb[:, 8:10], g3row[:, 0:2],
                            0.5 / NW, 0.5 * EPS_Q, op0=ALU.mult, op1=ALU.add)
            sv.inc(v.tensor_scalar(rows_sb[:, 11:13], g3row[:, 0:2],
                                   -0.5 / NW, -0.5 * EPS_Q, op0=ALU.mult, op1=ALU.add))
        sv_throw12 = sv.v

        @block.tensor
        def _(t):
            t.wait_ge(sv.h, sv_throw12)
            st.inc(t.matmul(ps_thr[:, 0:2], ones_row_f[:], rows_sb[:, 8:10],
                            start=True, stop=True))
            st.inc(t.matmul(ps_thr[:, 3:5], ones_row_f[:], rows_sb[:, 11:13],
                            start=True, stop=True))
        st_thr12 = st.v

        @block.vector
        def _(v):
            v.wait_ge(st.h, st_thr12)
            sv.inc(v.tensor_copy(thr_sb[:, 0:2], ps_thr[:, 0:2]))
            sv.inc(v.tensor_copy(thr_sb[:, 3:5], ps_thr[:, 3:5]))
        sv_thr12 = sv.v

        @block.scalar
        def _(s):
            for i in range(2 * HC, NG):
                s.wait_ge(sdG[i % 4].h, 16 * (i // 4 + 1))
                ss.inc(s.activation(gdump[:], gsl[i % 4][:], AF.Abs,
                                    accum_out=gpart[:, i:i + 1]))
                assert ss.v == g_red_mark(i)
        ss_gpart3 = ss.v

        def emit_gamma3_tail():
            @block.vector
            def _(v):
                v.wait_ge(ss.h, ss_gpart3)
                sv.inc(v.reduce_sum(gpart[:, 138:139], gpart[:, 2 * HC:NG],
                                    axis=AX.X))
            globals()
            return sv.v

        def emit_gamma3_thr():
            sv_gpart3 = emit_gamma3_tail()

            @block.tensor
            def _(t):
                t.wait_ge(sv.h, sv_gpart3)
                st.inc(t.matmul(ps_g3[:, 2:3], gpart[:, 138:139],
                                ones_col_f[:], start=True, stop=True))
            st_g3 = st.v

            @block.vector
            def _(v):
                v.wait_ge(st.h, st_g3)
                sv.inc(v.tensor_copy(g3row[:, 2:3], ps_g3[:, 2:3]))
                v.wait_ge(sv.h, sv.v)
                v.tensor_scalar(rows_sb[:, 10:11], g3row[:, 2:3],
                                0.5 / NW, 0.5 * EPS_Q, op0=ALU.mult, op1=ALU.add)
                sv.inc(v.tensor_scalar(rows_sb[:, 13:14], g3row[:, 2:3],
                                       -0.5 / NW, -0.5 * EPS_Q,
                                       op0=ALU.mult, op1=ALU.add))
            sv_throw3 = sv.v

            @block.tensor
            def _(t):
                t.wait_ge(sv.h, sv_throw3)
                st.inc(t.matmul(ps_thr[:, 2:3], ones_row_f[:], rows_sb[:, 10:11],
                                start=True, stop=True))
                st.inc(t.matmul(ps_thr[:, 5:6], ones_row_f[:], rows_sb[:, 13:14],
                                start=True, stop=True))
            st_thr3 = st.v

            @block.vector
            def _(v):
                v.wait_ge(st.h, st_thr3)
                sv.inc(v.tensor_copy(thr_sb[:, 2:3], ps_thr[:, 2:3]))
                sv.inc(v.tensor_copy(thr_sb[:, 5:6], ps_thr[:, 5:6]))
            thr_marks["thr12"] = sv_thr12
            thr_marks["thr3"] = sv.v

        def emit_ln(tag):
            esl = contextlib.ExitStack()
            sqr = [sb(f"sqr{tag}{i}", [128, T], BF16, esl) for i in range(2)]
            ln_tmp = sb(f"ln_tmp{tag}", [128, T], F32, esl)
            brep = sb(f"brep{tag}", [128, T], F32, esl)
            lnrow = sb(f"lnrow{tag}", [1, 6 * TBS], F32, esl)
            ps_sum = [psum(f"ps_sum{tag}{i}", [1, TBS], F32, esl) for i in range(2)]
            ps_sq = [psum(f"ps_sq{tag}{i}", [1, TBS], F32, esl) for i in range(2)]
            ps_bc = psum(f"ps_bc{tag}", [128, T], F32, esl)

            sv_base, st_base = sv.v, st.v

            def cast_mark(c):
                return sv_base + c + 1

            def stats_mark(c):
                return st_base + 4 * (c + 1)

            @block.vector
            def _(v):
                for c in range(DC):
                    if c >= 2:
                        v.wait_ge(st.h, stats_mark(c - 2))
                    v.tensor_copy(nT[:, hcol(c)], hT[:, hcol(c)])
                    v.wait_ge(sv.h, sv.v)
                    sv.inc(v.tensor_mul(sqr[c % 2][:], nT[:, hcol(c)], nT[:, hcol(c)]))
                    assert sv.v == cast_mark(c)

            @block.tensor
            def _(t):
                for c in range(DC):
                    t.wait_ge(sv.h, cast_mark(c))
                    for tb in range(2):
                        st.inc(t.matmul(ps_sum[tb][:], ones_col_b[:],
                                        nT[:, hcol(c, tb * TBS, (tb + 1) * TBS)],
                                        start=(c == 0), stop=(c == DC - 1)))
                        st.inc(t.matmul(ps_sq[tb][:], ones_col_b[:],
                                        sqr[c % 2][:, tb * TBS:(tb + 1) * TBS],
                                        start=(c == 0), stop=(c == DC - 1)))
                    assert st.v == stats_mark(c)
            st_stats = st.v

            @block.vector
            def _(v):
                v.wait_ge(st.h, st_stats)
                for tb in range(2):
                    o = 3 * tb * TBS
                    sv.inc(v.tensor_single_scalar(lnrow[:, o:o + TBS], ps_sum[tb][:],
                                                  1.0 / D, op=ALU.mult))
                    v.wait_ge(sv.h, sv.v)
                    sv.inc(v.tensor_mul(lnrow[:, o + TBS:o + 2 * TBS],
                                        lnrow[:, o:o + TBS], lnrow[:, o:o + TBS]))
                    sv.inc(v.tensor_scalar(lnrow[:, o + 2 * TBS:o + 3 * TBS],
                                           ps_sq[tb][:], 1.0 / D, EPS_LN,
                                           op0=ALU.mult, op1=ALU.add))
                    v.wait_ge(sv.h, sv.v)
                    sv.inc(v.tensor_sub(lnrow[:, o + 2 * TBS:o + 3 * TBS],
                                        lnrow[:, o + 2 * TBS:o + 3 * TBS],
                                        lnrow[:, o + TBS:o + 2 * TBS]))
            sv_var = sv.v

            @block.scalar
            def _(s):
                s.wait_ge(sv.h, sv_var)
                for tb in range(2):
                    o = 3 * tb * TBS
                    ss.inc(s.activation(lnrow[:, o + 2 * TBS:o + 3 * TBS],
                                        lnrow[:, o + 2 * TBS:o + 3 * TBS], AF.Ln))
                    s.wait_ge(ss.h, ss.v)
                    ss.inc(s.activation(lnrow[:, o + 2 * TBS:o + 3 * TBS],
                                        lnrow[:, o + 2 * TBS:o + 3 * TBS],
                                        AF.Exp, scale=-0.5))
            ss_r = ss.v

            @block.vector
            def _(v):
                v.wait_ge(ss.h, ss_r)
                for tb in range(2):
                    o = 3 * tb * TBS
                    sv.inc(v.scalar_tensor_tensor(
                        lnrow[:, o:o + TBS], lnrow[:, o:o + TBS], -1.0,
                        lnrow[:, o + 2 * TBS:o + 3 * TBS],
                        op0=ALU.mult, op1=ALU.mult))
            sv_rows = sv.v

            @block.tensor
            def _(t):
                t.wait_ge(sv.h, sv_rows)
                for tb in range(2):
                    o = 3 * tb * TBS
                    st.inc(t.matmul(ps_bc[:, tb * TBS:(tb + 1) * TBS], ones_row_f[:],
                                    lnrow[:, o + 2 * TBS:o + 3 * TBS],
                                    start=True, stop=True))
            st_bc1 = st.v

            @block.vector
            def _(v):
                v.wait_ge(st.h, st_bc1)
                sv.inc(v.tensor_copy(rrep[:], ps_bc[:]))
            sv_rrep = sv.v

            @block.tensor
            def _(t):
                t.wait_ge(sv.h, sv_rrep)
                for tb in range(2):
                    o = 3 * tb * TBS
                    st.inc(t.matmul(ps_bc[:, tb * TBS:(tb + 1) * TBS], ones_row_f[:],
                                    lnrow[:, o:o + TBS], start=True, stop=True))
            st_bc2 = st.v

            @block.vector
            def _(v):
                v.wait_ge(st.h, st_bc2)
                sv.inc(v.tensor_copy(brep[:], ps_bc[:]))
                v.wait_ge(sv.h, sv.v)
                for c in range(DC):
                    sv.inc(v.tensor_mul(ln_tmp[:], hT[:, hcol(c)], rrep[:]))
                    v.wait_ge(sv.h, sv.v)
                    sv.inc(v.tensor_add(nT[:, hcol(c)], ln_tmp[:], brep[:]))
                    v.wait_ge(sv.h, sv.v)

            barrier()
            esl.close()

        def emit_ffn(app):
            esl = contextlib.ExitStack()
            w12raw = [sb(f"w12r{app}{i}", [128, 2048], F32, esl) for i in range(2)]
            q12 = [sb(f"q12_{app}{i}", [128, 4096], BF16, esl) for i in range(2)]
            maskr = [sb(f"mask{app}{i}", [128, 1024], BF16, esl) for i in range(4)]
            if app == 1:
                w3A = [sb(f"w3A{i}", [128, 2048], F32, esl) for i in range(2)]
                q3A = [sb(f"q3A{i}", [128, 2048], BF16, esl) for i in range(2)]
                mask3 = [sb(f"mask3{i}", [128, 2048], BF16, esl) for i in range(2)]
            sig = [sb(f"sig{app}{i}", [128, TBS], BF16, esl) for i in range(4)]
            hid_ring = [sb(f"hidr{app}{i}", [128, TBS], BF16, esl) for i in range(4)]

            esa = contextlib.ExitStack()
            ps_x1 = [psum(f"psx1{app}{i}", [128, TBS], F32, esa) for i in range(4)]
            ps_x2 = [psum(f"psx2{app}{i}", [128, TBS], F32, esa) for i in range(4)]

            NU = 2 * HC
            sdA_b = [sdA[i].v for i in range(2)]
            sdS_b = [sdS[i].v for i in range(2)]
            sdH_b = [sdH[i].v for i in range(4)]
            sg_b, svq_b, ss_b, st_b = sg.v, sv.v, ss.v, st.v
            svh_b = svh.v
            sdB_bA = [sdB[i].v for i in range(2)]
            sdT_bA = [sdT[i].v for i in range(2)]

            def a_load_mark(u):
                return sdA_b[u % 2] + 32 * (u // 2 + 1)

            def a2_load_mark(c):
                return sdA_b[c % 2] + 32 * (c // 2 + 1)

            def a_quant_mark(u):
                c, hh = u // 2, u % 2
                return svq_b + 10 * c + 4 * (hh + 1)

            def a_w3q_mark(c):
                return svq_b + 10 * (c + 1)

            def a_w3load_mark(c):
                return sdB_bA[c % 2] + 16 * (c // 2 + 1)

            def a_w3scr_mark(c):
                return sdT_bA[c % 2] + 16 * (c // 2 + 1)

            def a_scr_mark(u):
                return sdS_b[u % 2] + 32 * (u // 2 + 1)

            def a_mm_mark(c):
                return st_b + 4 * (c + 1)

            def a_sig_mark(c, tb):
                return ss_b + 2 * c + tb + 1

            def a_hid_mark(c, tb):
                return svh_b + 2 * c + tb + 1

            def a_spill_mark(c, tb):
                k = 2 * c + tb
                return sdH_b[k % 4] + 16 * (k // 4 + 1)

            @block.sync
            def _(q):
                if app == 1:
                    for u in range(NU):
                        slot, c, hh = u % 2, u // 2, u % 2
                        if u >= 2:
                            q.wait_ge(sv.h, a_quant_mark(u - 2))
                        q.dma_start(out=w12raw[slot][:, 0:1024],
                                    in_=W1L[c, :, hh * 1024:(hh + 1) * 1024]
                                    ).then_inc(sdA[slot].h, 16)
                        q.dma_start(out=w12raw[slot][:, 1024:2048],
                                    in_=W2L[c, :, hh * 1024:(hh + 1) * 1024]
                                    ).then_inc(sdA[slot].h, 16)
                        sdA[slot].v += 32
                        assert sdA[slot].v == a_load_mark(u)
                        if hh == 0:
                            if c >= 2:
                                q.wait_ge(sv.h, a_w3q_mark(c - 2))
                            q.dma_start(out=w3A[c % 2][:], in_=W3L[c, :, :]
                                        ).then_inc(sdB[c % 2].h, 16)
                            sdB[c % 2].v += 16
                            assert sdB[c % 2].v == a_w3load_mark(c)
                        else:
                            if c >= 2:
                                cq = c - 2
                                q.wait_ge(sv.h, a_w3q_mark(cq))
                                q.dma_start(out=Q3S[cq, :, :], in_=q3A[cq % 2][:]
                                            ).then_inc(sdT[cq % 2].h, 16)
                                sdT[cq % 2].v += 16
                                assert sdT[cq % 2].v == a_w3scr_mark(cq)
                        if u >= 2:
                            up = u - 2
                            ps_, cp, hp = up % 2, up // 2, up % 2
                            q.wait_ge(sv.h, a_quant_mark(up))
                            q.dma_start(out=Q12S[cp, 0, :, hp * 1024:(hp + 1) * 1024],
                                        in_=q12[cp % 2][:, hp * 1024:(hp + 1) * 1024]
                                        ).then_inc(sdS[ps_].h, 16)
                            q.dma_start(out=Q12S[cp, 1, :, hp * 1024:(hp + 1) * 1024],
                                        in_=q12[cp % 2][:, 2048 + hp * 1024:
                                                       2048 + (hp + 1) * 1024]
                                        ).then_inc(sdS[ps_].h, 16)
                            sdS[ps_].v += 32
                            assert sdS[ps_].v == a_scr_mark(up)
                        ku = u - 8
                        if ku >= 0:
                            cs, tbs_ = ku // 2, ku % 2
                            q.wait_ge(svh.h, a_hid_mark(cs, tbs_))
                            q.dma_start(out=HIDS[cs, tbs_, :, :],
                                        in_=hid_ring[ku % 4][:]).then_inc(sdH[ku % 4].h, 16)
                            sdH[ku % 4].v += 16
                            assert sdH[ku % 4].v == a_spill_mark(cs, tbs_)
                    for cq in (HC - 2, HC - 1):
                        q.wait_ge(sv.h, a_w3q_mark(cq))
                        q.dma_start(out=Q3S[cq, :, :], in_=q3A[cq % 2][:]
                                    ).then_inc(sdT[cq % 2].h, 16)
                        sdT[cq % 2].v += 16
                        assert sdT[cq % 2].v == a_w3scr_mark(cq)
                    for up in range(NU - 2, NU):
                        ps_, cp, hp = up % 2, up // 2, up % 2
                        q.wait_ge(sv.h, a_quant_mark(up))
                        q.dma_start(out=Q12S[cp, 0, :, hp * 1024:(hp + 1) * 1024],
                                    in_=q12[cp % 2][:, hp * 1024:(hp + 1) * 1024]
                                    ).then_inc(sdS[ps_].h, 16)
                        q.dma_start(out=Q12S[cp, 1, :, hp * 1024:(hp + 1) * 1024],
                                    in_=q12[cp % 2][:, 2048 + hp * 1024:
                                                   2048 + (hp + 1) * 1024]
                                    ).then_inc(sdS[ps_].h, 16)
                        sdS[ps_].v += 32
                        assert sdS[ps_].v == a_scr_mark(up)
                    for ku in range(NU - 8, NU):
                        cs, tbs_ = ku // 2, ku % 2
                        q.wait_ge(svh.h, a_hid_mark(cs, tbs_))
                        q.dma_start(out=HIDS[cs, tbs_, :, :],
                                    in_=hid_ring[ku % 4][:]).then_inc(sdH[ku % 4].h, 16)
                        sdH[ku % 4].v += 16
                        assert sdH[ku % 4].v == a_spill_mark(cs, tbs_)
                else:
                    for c in range(HC):
                        slot = c % 2
                        if c >= 2:
                            q.wait_ge(st.h, a_mm_mark(c - 2))
                        q.dma_start(out=q12[slot][:, 0:2048], in_=Q12S[c, 0, :, :]
                                    ).then_inc(sdA[slot].h, 16)
                        q.dma_start(out=q12[slot][:, 2048:4096], in_=Q12S[c, 1, :, :]
                                    ).then_inc(sdA[slot].h, 16)
                        sdA[slot].v += 32
                        assert sdA[slot].v == a2_load_mark(c)
                        ku = 2 * c - 8
                        for k in (ku, ku + 1):
                            if 0 <= k:
                                cs, tbs_ = k // 2, k % 2
                                q.wait_ge(svh.h, a_hid_mark(cs, tbs_))
                                q.dma_start(out=HIDS[cs, tbs_, :, :],
                                            in_=hid_ring[k % 4][:]
                                            ).then_inc(sdH[k % 4].h, 16)
                                sdH[k % 4].v += 16
                                assert sdH[k % 4].v == a_spill_mark(cs, tbs_)
                    for k in range(2 * HC - 8, 2 * HC):
                        cs, tbs_ = k // 2, k % 2
                        q.wait_ge(svh.h, a_hid_mark(cs, tbs_))
                        q.dma_start(out=HIDS[cs, tbs_, :, :], in_=hid_ring[k % 4][:]
                                    ).then_inc(sdH[k % 4].h, 16)
                        sdH[k % 4].v += 16
                        assert sdH[k % 4].v == a_spill_mark(cs, tbs_)



            @block.tensor
            def _(t):
                for c in range(HC):
                    slot = c % 2
                    if app == 1:
                        t.wait_ge(sv.h, a_quant_mark(2 * c + 1))
                    else:
                        t.wait_ge(sdA[slot].h, a2_load_mark(c))
                    for mat in range(2):
                        psl = (ps_x1, ps_x2)[mat]
                        for tb in range(2):
                            bank = psl[2 * tb + (c % 2)]
                            if c >= 2:
                                if mat == 0:
                                    t.wait_ge(ss.h, a_sig_mark(c - 2, tb))
                                else:
                                    t.wait_ge(svh.h, a_hid_mark(c - 2, tb))
                            for ic in range(IC):
                                mm = t.matmul(
                                    bank[:],
                                    q12[slot][:, mat * 2048 + ic * 128:
                                              mat * 2048 + ic * 128 + 128],
                                    nT[:, hcol(ic, tb * TBS, (tb + 1) * TBS)],
                                    start=(ic == 0), stop=(ic == IC - 1))
                            st.inc(mm)
                    assert st.v == a_mm_mark(c)

            @block.scalar
            def _(s):
                for c in range(HC):
                    for tb in range(2):
                        s.wait_ge(st.h, st_b + 4 * c + tb + 1)
                        if c >= 2:
                            s.wait_ge(svh.h, a_hid_mark(c - 2, tb))
                        ss.inc(s.activation(sig[2 * tb + (c % 2)][:],
                                            ps_x1[2 * tb + (c % 2)][:], AF.Sigmoid))
                        assert ss.v == a_sig_mark(c, tb)

            @block.vector
            def _(v):
                def emit_quant(u):
                    slot, c, hh = u % 2, u // 2, u % 2
                    if u == 0 and app == 1:
                        v.wait_ge(sv.h, thr_marks["thr12"])
                    v.wait_ge(sdA[slot].h, a_load_mark(u))
                    if u >= 4:
                        v.wait_ge(sdS[(u - 4) % 2].h, a_scr_mark(u - 4))
                    if c >= 2 and hh == 0:
                        v.wait_ge(st.h, a_mm_mark(c - 2))
                    for mat in range(2):
                        sv.inc(v.tensor_single_scalar(
                            maskr[(2 * u + mat) % 4][:],
                            w12raw[slot][:, mat * 1024:(mat + 1) * 1024],
                            thr_sb[:, 3 + mat:4 + mat], op=ALU.is_lt))
                    v.wait_ge(sv.h, sv.v)
                    for mat in range(2):
                        sv.inc(v.scalar_tensor_tensor(
                            q12[c % 2][:, mat * 2048 + hh * 1024:
                                       mat * 2048 + (hh + 1) * 1024],
                            w12raw[slot][:, mat * 1024:(mat + 1) * 1024],
                            thr_sb[:, mat:mat + 1], maskr[(2 * u + mat) % 4][:],
                            op0=ALU.is_gt, op1=ALU.subtract))
                    assert sv.v == a_quant_mark(u)

                def emit_hid(c, tb):
                    k = 2 * c + tb
                    v.wait_ge(ss.h, a_sig_mark(c, tb))
                    v.wait_ge(st.h, st_b + 4 * c + 2 + tb + 1)
                    if k >= 4:
                        v.wait_ge(sdH[k % 4].h,
                                  a_spill_mark((k - 4) // 2, (k - 4) % 2))
                    svh.inc(v.scalar_tensor_tensor(
                        hid_ring[k % 4][:], ps_x2[2 * tb + (c % 2)][:], 1.0,
                        sig[2 * tb + (c % 2)][:], op0=ALU.mult, op1=ALU.mult))
                    assert svh.v == a_hid_mark(c, tb)

                for c in range(HC):
                    if app == 1:
                        emit_quant(2 * c)
                        emit_quant(2 * c + 1)
                        if c == 0:
                            v.wait_ge(sv.h, thr_marks["thr3"])
                        v.wait_ge(sdB[c % 2].h, a_w3load_mark(c))
                        if c >= 4:
                            v.wait_ge(sdT[(c - 4) % 2].h, a_w3scr_mark(c - 4))
                        sv.inc(v.tensor_single_scalar(
                            mask3[c % 2][:], w3A[c % 2][:],
                            thr_sb[:, 5:6], op=ALU.is_lt))
                        v.wait_ge(sv.h, sv.v)
                        sv.inc(v.scalar_tensor_tensor(
                            q3A[c % 2][:], w3A[c % 2][:], thr_sb[:, 2:3],
                            mask3[c % 2][:], op0=ALU.is_gt, op1=ALU.subtract))
                        assert sv.v == a_w3q_mark(c)
                    if c >= 2:
                        emit_hid(c - 2, 0)
                        emit_hid(c - 2, 1)
                for c in (HC - 2, HC - 1):
                    emit_hid(c, 0)
                    emit_hid(c, 1)

            esa.close()
            q3 = [sb(f"q3_{app}{i}", [128, 1024], BF16, esl) for i in range(4)]
            hidl = [sb(f"hidl{app}{i}", [128, TBS], BF16, esl) for i in range(4)]
            esb = contextlib.ExitStack()

            sdB_b = [sdB[i].v for i in range(4)]
            sdR_b = [sdR[i].v for i in range(4)]
            sv3_b, st3_b = sv.v, st.v
            NSW = 4

            def b_load_mark(i):
                return sdB_b[i % 4] + 16 * (i // 4 + 1)

            def b_hid_mark(i):
                return sdR_b[i % 4] + 16 * (i // 4 + 1)

            def b_mm_mark(i):
                return st3_b + i + 1

            def b_evac_mark(swp):
                return sv3_b + 8 * (swp + 1)

            def b_evac_bank_mark(swp, m8):
                return sv3_b + 8 * swp + m8 + 1

            @block.sync
            def _(q):
                for swp in range(NSW):
                    tb, s_ = swp // 2, swp % 2
                    for c in range(HC):
                        i = swp * HC + c
                        slot = i % 4
                        if i >= 4:
                            q.wait_ge(st.h, b_mm_mark(i - 4))
                        if app == 1 and swp == 0:
                            q.wait_ge(sdT[c % 2].h, a_w3scr_mark(c))
                        q.dma_start(out=q3[slot][:],
                                    in_=Q3S[c, :, s_ * 1024:(s_ + 1) * 1024]
                                    ).then_inc(sdB[slot].h, 16)
                        sdB[slot].v += 16
                        assert sdB[slot].v == b_load_mark(i)

            @block.gpsimd
            def _(g):
                for swp in range(NSW):
                    tb, s_ = swp // 2, swp % 2
                    for c in range(HC):
                        i = swp * HC + c
                        if i >= 4:
                            g.wait_ge(st.h, b_mm_mark(i - 4))
                        g.wait_ge(sdH[(2 * c + tb) % 4].h, a_spill_mark(c, tb))
                        g.dma_start(out=hidl[i % 4][:], in_=HIDS[c, tb, :, :]
                                    ).then_inc(sdR[i % 4].h, 16)
                        sdR[i % 4].v += 16
                        assert sdR[i % 4].v == b_hid_mark(i)

            ps_o = [psum(f"pso{app}{i}", [128, TBS], F32, esb) for i in range(8)]

            @block.vector
            def _(v):
                for swp in range(NSW):
                    tb, s_ = swp // 2, swp % 2
                    v.wait_ge(st.h, b_mm_mark(swp * HC + HC - 1))
                    for m8 in range(8):
                        m = s_ * 8 + m8
                        sv.inc(v.scalar_tensor_tensor(
                            hT[:, hcol(m, tb * TBS, (tb + 1) * TBS)],
                            ps_o[m8][:], 0.5,
                            hT[:, hcol(m, tb * TBS, (tb + 1) * TBS)],
                            op0=ALU.mult, op1=ALU.add))
                    assert sv.v == b_evac_mark(swp)

            @block.tensor
            def _(t):
                for swp in range(NSW):
                    tb, s_ = swp // 2, swp % 2
                    for c in range(HC):
                        i = swp * HC + c
                        slot = i % 4
                        t.wait_ge(sdB[slot].h, b_load_mark(i))
                        t.wait_ge(sdR[i % 4].h, b_hid_mark(i))
                        if c == 0 and swp == 0:
                            t.wait_ge(ss.h, a_sig_mark(HC - 1, 1))
                            t.wait_ge(svh.h, a_hid_mark(HC - 1, 1))
                        for m8 in range(8):
                            if c == 0 and swp > 0:
                                t.wait_ge(sv.h, b_evac_bank_mark(swp - 1, m8))
                            mm = t.matmul(ps_o[m8][:],
                                          q3[slot][:, m8 * 128:(m8 + 1) * 128],
                                          hidl[i % 4][:],
                                          start=(c == 0), stop=(c == HC - 1))
                        st.inc(mm)
                        assert st.v == b_mm_mark(i)

            barrier()
            esb.close()
            esl.close()

        def emit_attn():
            esl = contextlib.ExitStack()
            kT = sb("kT", [128, NKV * 2048], BF16, esl)
            v_tm = sb("v_tm", [128, NKV * 2048], BF16, esl)
            wring = [sb(f"wring{i}", [128, 2048], BF16, esl) for i in range(2)]
            den_b = sb("den_b", [1, TBS], BF16, esl)
            den_f = sb("den_f", [1, TBS], F32, esl)
            negc_sb = sb("negc_sb", [128, 1], F32, esl)
            esvt = contextlib.ExitStack()
            vT = sb("vT", [128, NKV * 2048], BF16, esvt)
            sqa = sb("sqa", [128, 2048], BF16, esvt)

            ps_qp = psum("ps_qp", [128, T], F32, esl)

            sgq_b = [sgq[i].v for i in range(2)]
            st_a, ss_a, sv_a, sg_a = st.v, ss.v, sv.v, sg.v

            ring_src = [(WKL, 0), (WKL, 1), (WVL, 0), (WVL, 1)] \
                + [(WQL, i) for i in range(NH)] + [(WOL, i) for i in range(DC)]

            def ring_mark(i):
                return sgq_b[i % 2] + 16 * (i // 2 + 1)

            A0 = st_a
            order = []
            for jp in range(8):
                order.append(("sc", jp))
                if jp >= 1:
                    order.append(("pv", jp - 1))
            order.append(("pv", 7))
            sc_off, pv_off = [0] * 8, [0] * 8
            for idx_, (kind, jp) in enumerate(order):
                (sc_off if kind == "sc" else pv_off)[jp] = idx_ + 1

            def kv_mm_mark(oc, tb=1):
                return A0 + 2 * oc + tb + 1

            def vt_mark(grp):
                return A0 + 8 + grp + 1

            def kn_mm_mark(g_, k_):
                return A0 + 16 + 4 * g_ + k_ + 1

            QP0 = A0 + 24

            def qproj_mark(hh, tb=1):
                return QP0 + 4 * hh + tb + 1

            def qnA_mark(hh):
                return QP0 + 4 * hh + 3

            def qnB_mark(hh):
                return QP0 + 4 * hh + 4

            negc_bc_mark = QP0 + 65
            H0 = QP0 + 65

            def hb(idx):
                return H0 + 17 * idx

            def sc_mark(idx, jp):
                return hb(idx) + sc_off[jp]

            def pv_mark(idx, jp):
                return hb(idx) + pv_off[jp]

            def dbc_mark(idx):
                return hb(idx) + 17

            OP0 = H0 + 17 * 32

            def oproj_mark(m, tb):
                return OP0 + 2 * m + tb + 1

            SS0 = ss_a + 8

            def ss_qh_mark(hh, tb):
                return SS0 + 2 * hh + tb + 1

            ss_negc_mark = SS0 + 34

            def exp_mark(idx, jp):
                return SS0 + 34 + 10 * idx + jp + 1

            def ss_recip_mark(idx):
                return SS0 + 34 + 10 * idx + 10

            SV0 = sv_a + 8 + 13

            def qn_base(hh):
                return SV0 + 5 * hh

            NC0 = SV0 + 80
            sv_negc_mark = NC0 + 4
            HD0 = NC0 + 4

            def sv_rrep_mark(idx):
                return HD0 + 2 * idx + 1

            def sv_ao_mark(idx):
                return HD0 + 2 * idx + 2

            OE0 = HD0 + 64

            def sv_oevac_mark(m, tb):
                return OE0 + 2 * m + tb + 1

            def emit_ring(g, lo, hi):
                for i in range(lo, hi):
                    W, idx = ring_src[i]
                    slot = i % 2
                    if i >= 2:
                        j = i - 2
                        if j < 4:
                            g.wait_ge(st.h, kv_mm_mark(j))
                        elif j < 4 + NH:
                            g.wait_ge(st.h, qproj_mark(j - 4))
                        else:
                            g.wait_ge(st.h, oproj_mark(j - 4 - NH, 1))
                    g.dma_start(out=wring[slot][:], in_=W[idx, :, :]
                                ).then_inc(sgq[slot].h, 16)
                    sgq[slot].v += 16
                    assert sgq[slot].v == ring_mark(i)

            @block.gpsimd
            def _(g):
                emit_ring(g, 0, 4)

            @block.tensor
            def _(t):
                for oc in range(4):
                    slot = oc % 2
                    t.wait_ge(sgq[slot].h, ring_mark(oc))
                    for tb in range(2):
                        if oc >= 2:
                            t.wait_ge(ss.h, ss_a + 2 * (oc - 2) + tb + 1)
                        for ic in range(IC):
                            mm = t.matmul(ps_qp[:, tb * TBS:(tb + 1) * TBS],
                                          wring[slot][:, ic * 128:(ic + 1) * 128],
                                          nT[:, hcol(ic, tb * TBS, (tb + 1) * TBS)],
                                          start=(ic == 0), stop=(ic == IC - 1))
                        st.inc(mm)
                    assert st.v == kv_mm_mark(oc)

            @block.scalar
            def _(s):
                for oc in range(4):
                    g_, isv = oc % 2, oc // 2
                    dstT = vT if isv else kT
                    bias = (bv_sb if isv else bk_sb)[:, g_:g_ + 1]
                    for tb in range(2):
                        s.wait_ge(st.h, A0 + 2 * oc + tb + 1)
                        ss.inc(s.activation(
                            dstT[:, g_ * 2048 + tb * TBS:g_ * 2048 + (tb + 1) * TBS],
                            ps_qp[:, tb * TBS:(tb + 1) * TBS],
                            AF.Identity, bias=bias))
            assert ss.v == ss_a + 8

            @block.gpsimd
            def _(g):
                g.wait_ge(sd_b.h, 96)
                g.reg_load(rh, half_sb[:1, :1])
                g.reg_load(rn, nonce_sb[:1, :1])
                g.reg_mov(rit, 0)
                g.wait_ge(ss.h, ss_a + 8)
                for r in range(4):
                    isv, g_ = r // 2, r % 2
                    src = (vT if isv else kT)[:, g_ * 2048:g_ * 2048 + T]
                    with g.If_eq(rh, 0):
                        g.dma_start(out=XKV[0, r, :, :], in_=src).then_inc(sgkv.h, 16)
                    with g.Else():
                        g.dma_start(out=XKV[1, r, :, :], in_=src).then_inc(sgkv.h, 16)
                    sgkv.v += 16
                g.wait_ge(sgkv.h, sgkv.v)
                with g.If_eq(rh, 0):
                    g.dma_start(out=XKF[0:1, :], in_=nonce_sb[:]).then_inc(sgfl.h, 16)
                with g.Else():
                    g.dma_start(out=XKF[1:2, :], in_=nonce_sb[:]).then_inc(sgfl.h, 16)
                sgfl.v += 16
                g.wait_ge(sgfl.h, sgfl.v)
                g.reg_mov(rp, 1)
                with g.While(rp):
                    with g.If_eq(rh, 0):
                        g.dma_start(out=flag_sb[:], in_=XKF[1:2, :]).then_inc(sgpl.h, 16)
                    with g.Else():
                        g.dma_start(out=flag_sb[:], in_=XKF[0:1, :]).then_inc(sgpl.h, 16)
                    g.reg_add(rit, rit, 16)
                    g.wait_ge(sgpl.h, rit)
                    g.reg_load(rf, flag_sb[:1, :1])
                    g.reg_sub(rp, rf, rn)
                for r in range(4):
                    isv, g_ = r // 2, r % 2
                    dst = (vT if isv else kT)[:, g_ * 2048 + T:(g_ + 1) * 2048]
                    with g.If_eq(rh, 0):
                        g.dma_start(out=dst, in_=XKV[1, r, :, :]).then_inc(sgin.h, 16)
                    with g.Else():
                        g.dma_start(out=dst, in_=XKV[0, r, :, :]).then_inc(sgin.h, 16)
                    sgin.v += 16
            sgin_kv = sgin.v
            sgkv_out = sgkv.v

            @block.gpsimd
            def _(g):
                emit_ring(g, 4, len(ring_src))

            estr = contextlib.ExitStack()
            ps_tr = [psum(f"ps_tr{i}", [128, 512], BF16, estr) for i in range(2)]

            @block.tensor
            def _(t):
                t.wait_ge(sgb.h, 16)
                grporder = [0, 1, 4, 5, 2, 3, 6, 7]
                for k_, grp in enumerate(grporder):
                    g_, jq = grp // 4, grp % 4
                    if k_ == 0:
                        t.wait_ge(ss.h, ss_a + 8)
                    if k_ == 4:
                        t.wait_ge(sgin.h, sgin_kv)
                    if k_ >= 2:
                        t.wait_ge(sv.h, sv_a + k_ - 1)
                    for jj in range(4):
                        j = 4 * jq + jj
                        tr = t.transpose(ps_tr[k_ % 2][:, jj * 128:(jj + 1) * 128],
                                         vT[:, g_ * 2048 + j * 128:
                                            g_ * 2048 + (j + 1) * 128],
                                         idb[:])
                    st.inc(tr)
                    assert st.v == A0 + 8 + k_ + 1

            @block.vector
            def _(v):
                grporder = [0, 1, 4, 5, 2, 3, 6, 7]
                for k_, grp in enumerate(grporder):
                    g_, jq = grp // 4, grp % 4
                    v.wait_ge(st.h, A0 + 8 + k_ + 1)
                    sv.inc(v.tensor_copy(
                        v_tm[:, g_ * 2048 + jq * 512:g_ * 2048 + (jq + 1) * 512],
                        ps_tr[k_ % 2][:]))
            assert sv.v == sv_a + 8

            esvt.close()
            qT = sb("qT", [128, NH * T], BF16, esl)
            sqa1k = sb("sqa1k", [128, T], BF16, esl)
            aoT = sb("aoT", [128, DC * T], BF16, esl)
            probs = [sb(f"probs{i}", [128, T], BF16, esl) for i in range(3)]

            estr.close()
            ps_sc = [psum(f"ps_sc{i}", [128, T], F32, esl) for i in range(2)]
            ps_po = [psum(f"ps_po{i}", [128, TBS], F32, esl) for i in range(2)]

            def kn_sq_mark(g_):
                return sv_a + 8 + 6 * g_ + 1

            def kn_col_mark(g_, k_):
                return sv_a + 8 + 6 * g_ + 1 + k_ + 1

            @block.vector
            def _(v):
                for g_ in range(2):
                    v.wait_ge(sgin.h, sgin_kv)
                    if g_ == 1:
                        v.wait_ge(st.h, kn_mm_mark(0, 3))
                    sv.inc(v.tensor_mul(sqa[:], kT[:, g_ * 2048:(g_ + 1) * 2048],
                                        kT[:, g_ * 2048:(g_ + 1) * 2048]))
                    assert sv.v == kn_sq_mark(g_)
                    for k_ in range(4):
                        v.wait_ge(st.h, kn_mm_mark(g_, k_))
                        sv.inc(v.reduce_max(rows_sb[:, 44 + k_:45 + k_],
                                            ps_po[1][0:1, 0:TBS], axis=AX.X))
                        assert sv.v == kn_col_mark(g_, k_)
                    v.wait_ge(sv.h, sv.v)
                    sv.inc(v.reduce_max(kmax2_sb[:, g_:g_ + 1], rows_sb[:, 44:48],
                                        axis=AX.X))
                sv.inc(v.memset(rows_sb[:, 40:41], 0.0))
                assert sv.v == SV0

            @block.tensor
            def _(t):
                for g_ in range(2):
                    t.wait_ge(sv.h, kn_sq_mark(g_))
                    for k_ in range(4):
                        if (g_, k_) != (0, 0):
                            pk = (g_, k_ - 1) if k_ else (g_ - 1, 3)
                            t.wait_ge(sv.h, kn_col_mark(*pk))
                        st.inc(t.matmul(ps_po[1][0:1, 0:TBS], ones_col_b[:],
                                        sqa[:, k_ * TBS:(k_ + 1) * TBS],
                                        start=True, stop=True))
                        assert st.v == kn_mm_mark(g_, k_)


            @block.tensor
            def _(t):
                for hh in range(NH):
                    slot = (4 + hh) % 2
                    t.wait_ge(sgq[slot].h, ring_mark(4 + hh))
                    for tb in range(2):
                        if hh >= 1:
                            t.wait_ge(ss.h, ss_qh_mark(hh - 1, tb))
                        else:
                            t.wait_ge(ss.h, ss_a + 8)
                        for ic in range(IC):
                            mm = t.matmul(ps_qp[:, tb * TBS:(tb + 1) * TBS],
                                          wring[slot][:, ic * 128:(ic + 1) * 128],
                                          nT[:, hcol(ic, tb * TBS, (tb + 1) * TBS)],
                                          start=(ic == 0), stop=(ic == IC - 1))
                        st.inc(mm)
                        assert st.v == qproj_mark(hh, tb)
                    t.wait_ge(sv.h, qn_base(hh) + 1)
                    st.inc(t.matmul(ps_po[1][0:1, 0:TBS], ones_col_b[:],
                                    sqa1k[:, 0:TBS], start=True, stop=True))
                    t.wait_ge(sv.h, qn_base(hh) + 2)
                    st.inc(t.matmul(ps_po[1][0:1, 0:TBS], ones_col_b[:],
                                    sqa1k[:, TBS:2 * TBS], start=True, stop=True))
                    assert st.v == qnB_mark(hh)
                t.wait_ge(ss.h, ss_negc_mark)
                st.inc(t.matmul(ps_po[0][:, 0:1], negones_row_f[:],
                                rows_sb[:, 36:37], start=True, stop=True))
                assert st.v == negc_bc_mark

            @block.scalar
            def _(s):
                for hh in range(NH):
                    for tb in range(2):
                        s.wait_ge(st.h, qproj_mark(hh, tb))
                        if hh == 0 and tb == 0:
                            s.wait_ge(st.h, vt_mark(7))
                            s.wait_ge(sgkv.h, sgkv_out)
                        ss.inc(s.activation(qT[:, hh * T + tb * TBS:
                                               hh * T + (tb + 1) * TBS],
                                            ps_qp[:, tb * TBS:(tb + 1) * TBS],
                                            AF.Identity, bias=bq_sb[:, hh:hh + 1]))
                        assert ss.v == ss_qh_mark(hh, tb)
                s.wait_ge(sv.h, NC0 + 3)
                ss.inc(s.activation(rows_sb[:, 36:37], rows_sb[:, 38:39], AF.Ln))
                s.wait_ge(ss.h, ss.v)
                ss.inc(s.activation(rows_sb[:, 36:37], rows_sb[:, 36:37],
                                    AF.Exp, scale=0.5))
                assert ss.v == ss_negc_mark
                for idx in range(32):
                    hh, tb = idx // 2, idx % 2
                    for jp in range(8):
                        par = jp % 2
                        s.wait_ge(st.h, sc_mark(idx, jp))
                        if idx == 0 and jp == 0:
                            s.wait_ge(sv.h, sv_negc_mark)
                        gj = 8 * idx + jp
                        if gj >= 3:
                            pidx, pjp = divmod(gj - 3, 8)
                            s.wait_ge(st.h, pv_mark(pidx, pjp))
                        ss.inc(s.activation(probs[gj % 3][:], ps_sc[par][:],
                                            AF.Exp, bias=negc_sb[:, 0:1],
                                            scale=SCALE))
                        assert ss.v == exp_mark(idx, jp)
                    s.wait_ge(st.h, pv_mark(idx, 7))
                    ss.inc(s.activation(den_f[:], ps_qp[0:1, TBS:2 * TBS], AF.Ln))
                    s.wait_ge(ss.h, ss.v)
                    ss.inc(s.activation(den_b[:], den_f[:], AF.Exp, scale=-1.0))
                    assert ss.v == ss_recip_mark(idx)

            @block.vector
            def _(v):
                for hh in range(NH):
                    v.wait_ge(ss.h, ss_qh_mark(hh, 1))
                    if hh >= 1:
                        v.wait_ge(st.h, qnB_mark(hh - 1))
                    sv.inc(v.tensor_mul(sqa1k[:], qT[:, hh * T:(hh + 1) * T],
                                        qT[:, hh * T:(hh + 1) * T]))
                    assert sv.v == qn_base(hh) + 1
                    v.wait_ge(st.h, qnA_mark(hh))
                    sv.inc(v.reduce_max(rows_sb[:, 32:33], ps_po[1][0:1, 0:TBS],
                                        axis=AX.X))
                    v.wait_ge(st.h, qnB_mark(hh))
                    sv.inc(v.reduce_max(rows_sb[:, 33:34], ps_po[1][0:1, 0:TBS],
                                        axis=AX.X))
                    v.wait_ge(sv.h, sv.v)
                    sv.inc(v.tensor_max(rows_sb[:, 40:41], rows_sb[:, 40:41],
                                        rows_sb[:, 32:33]))
                    v.wait_ge(sv.h, sv.v)
                    sv.inc(v.tensor_max(rows_sb[:, 40:41], rows_sb[:, 40:41],
                                        rows_sb[:, 33:34]))
                    assert sv.v == qn_base(hh) + 5
                sv.inc(v.reduce_max(rows_sb[:, 41:42], kmax2_sb[:, 0:2], axis=AX.X))
                v.wait_ge(sv.h, sv.v)
                sv.inc(v.tensor_mul(rows_sb[:, 42:43], rows_sb[:, 40:41],
                                    rows_sb[:, 41:42]))
                v.wait_ge(sv.h, sv.v)
                sv.inc(v.tensor_single_scalar(rows_sb[:, 38:39], rows_sb[:, 42:43],
                                              1.0 / HD, op=ALU.mult))
                assert sv.v == NC0 + 3
                v.wait_ge(st.h, negc_bc_mark)
                sv.inc(v.tensor_copy(negc_sb[:], ps_po[0][:, 0:1]))
                assert sv.v == sv_negc_mark
                for idx in range(32):
                    hh, tb = idx // 2, idx % 2
                    v.wait_ge(st.h, dbc_mark(idx))
                    sv.inc(v.tensor_copy(rrep[:, 0:TBS], ps_po[(idx + 1) % 2][:, 0:TBS]))
                    assert sv.v == sv_rrep_mark(idx)
                    v.wait_ge(sv.h, sv.v)
                    sv.inc(v.tensor_mul(
                        aoT[:, hcol(hh, tb * TBS, (tb + 1) * TBS)],
                        ps_po[idx % 2][:], rrep[:, 0:TBS]))
                    assert sv.v == sv_ao_mark(idx)
                for m in range(DC):
                    for tb in range(2):
                        v.wait_ge(st.h, oproj_mark(m, tb))
                        sv.inc(v.scalar_tensor_tensor(
                            hT[:, hcol(m, tb * TBS, (tb + 1) * TBS)],
                            ps_qp[:, tb * TBS:(tb + 1) * TBS],
                            bo_sb[:, m:m + 1],
                            hT[:, hcol(m, tb * TBS, (tb + 1) * TBS)],
                            op0=ALU.add, op1=ALU.add))
                        assert sv.v == sv_oevac_mark(m, tb)

            @block.tensor
            def _(t):
                for idx in range(32):
                    hh, tb = idx // 2, idx % 2
                    g_ = hh // 8
                    for kind, jp in order:
                        if kind == "sc":
                            par = jp % 2
                            if jp >= 2:
                                t.wait_ge(ss.h, exp_mark(idx, jp - 2))
                            elif idx >= 1:
                                t.wait_ge(ss.h, exp_mark(idx - 1, 6 + jp))
                            if jp == 0:
                                t.wait_ge(ss.h, ss_qh_mark(hh, tb))
                            for jj in range(2):
                                j = 2 * jp + jj
                                mm = t.matmul(
                                    ps_sc[par][:, jj * TBS:(jj + 1) * TBS],
                                    kT[:, g_ * 2048 + j * 128:
                                       g_ * 2048 + (j + 1) * 128],
                                    qT[:, hh * T + tb * TBS:hh * T + (tb + 1) * TBS],
                                    start=True, stop=True)
                            st.inc(mm)
                            assert st.v == sc_mark(idx, jp)
                        else:
                            t.wait_ge(ss.h, exp_mark(idx, jp))
                            if jp == 0:
                                if idx == 0:
                                    t.wait_ge(sv.h, sv_negc_mark)
                                if idx >= 2:
                                    t.wait_ge(sv.h, sv_ao_mark(idx - 2))
                                if idx >= 1:
                                    t.wait_ge(ss.h, ss_recip_mark(idx - 1))
                                    t.wait_ge(sv.h, sv_rrep_mark(idx - 1))
                            gslot = (8 * idx + jp) % 3
                            for jj in range(2):
                                j = 2 * jp + jj
                                t.matmul(ps_po[idx % 2][:],
                                         v_tm[:, g_ * 2048 + j * 128:
                                              g_ * 2048 + (j + 1) * 128],
                                         probs[gslot][:, jj * TBS:(jj + 1) * TBS],
                                         start=(jp == 0 and jj == 0),
                                         stop=(jp == 7 and jj == 1))
                            for jj in range(2):
                                mm = t.matmul(ps_qp[0:1, TBS:2 * TBS], ones_col_b[:],
                                              probs[gslot][:, jj * TBS:
                                                           (jj + 1) * TBS],
                                              start=(jp == 0 and jj == 0),
                                              stop=(jp == 7 and jj == 1))
                            st.inc(mm)
                            assert st.v == pv_mark(idx, jp)
                    t.wait_ge(ss.h, ss_recip_mark(idx))
                    if idx >= 1:
                        t.wait_ge(sv.h, sv_ao_mark(idx - 1))
                    st.inc(t.matmul(ps_po[(idx + 1) % 2][:, 0:TBS], ones_row_b[:],
                                    den_b[:], start=True, stop=True))
                    assert st.v == dbc_mark(idx)
                for m in range(DC):
                    slot = (4 + NH + m) % 2
                    t.wait_ge(sgq[slot].h, ring_mark(4 + NH + m))
                    if m == 0:
                        t.wait_ge(sv.h, sv_ao_mark(31))
                        t.wait_ge(ss.h, ss_qh_mark(NH - 1, 1))
                    for tb in range(2):
                        if m >= 1:
                            t.wait_ge(sv.h, sv_oevac_mark(m - 1, tb))
                        for ac in range(IC):
                            mm = t.matmul(ps_qp[:, tb * TBS:(tb + 1) * TBS],
                                          wring[slot][:, ac * 128:(ac + 1) * 128],
                                          aoT[:, hcol(ac, tb * TBS, (tb + 1) * TBS)],
                                          start=(ac == 0), stop=(ac == IC - 1))
                        st.inc(mm)
                        assert st.v == oproj_mark(m, tb)

            barrier()
            esl.close()

        emit_ln("1")
        emit_gamma3_thr()
        esg.close()
        emit_ffn(1)
        emit_ln("2")
        emit_attn()
        emit_ln("3")
        emit_ffn(2)

        @block.sync
        def _(q):
            for c in range(DC):
                q.dma_start(out=OUT[c * 128:(c + 1) * 128, :], in_=hT[:, hcol(c)]
                            ).then_inc(sd_out.h, 16)
            sd_out.v += 16 * DC
            q.wait_ge(sd_out.h, sd_out.v)

    es.close()
    return nc



def _layout_w(w):
    O, I = w.shape
    return np.ascontiguousarray(
        w.reshape(O // 128, 128, I // 128, 128).transpose(0, 3, 2, 1)
    ).reshape(O // 128, 128, I)


def kernel(L0_attn_x, L0_attn_w_q, L0_attn_b_q, L0_attn_w_k, L0_attn_b_k,
           L0_attn_w_v, L0_attn_b_v, L0_attn_w_o, L0_attn_b_o,
           L0_ffn_w1, L0_ffn_w2, L0_ffn_w3):
    x = np.asarray(L0_attn_x, dtype=np.float32)
    W1 = _layout_w(np.asarray(L0_ffn_w1, np.float32))
    W2 = _layout_w(np.asarray(L0_ffn_w2, np.float32))
    W3 = np.ascontiguousarray(np.asarray(L0_ffn_w3, np.float32).T
                              ).reshape(HC, 128, 2048)
    WQ = _layout_w(np.asarray(L0_attn_w_q, np.float32))
    WK = _layout_w(np.asarray(L0_attn_w_k, np.float32))
    WV = _layout_w(np.asarray(L0_attn_w_v, np.float32))
    WO = _layout_w(np.asarray(L0_attn_w_o, np.float32))
    BQa = np.ascontiguousarray(np.asarray(L0_attn_b_q, np.float32).reshape(16, 128).T)
    BKa = np.ascontiguousarray(np.asarray(L0_attn_b_k, np.float32).reshape(2, 128).T)
    BVa = np.ascontiguousarray(np.asarray(L0_attn_b_v, np.float32).reshape(2, 128).T)
    BOa = np.ascontiguousarray(np.asarray(L0_attn_b_o, np.float32).reshape(16, 128).T)
    IDENTa = np.eye(128, dtype=np.float32)
    nonce = np.array([[np.random.randint(1, 2 ** 30)]], np.int32)

    in_maps = []
    for core in range(8):
        b, h = core // 2, core % 2
        xt = np.ascontiguousarray(x[b, h * 1024:(h + 1) * 1024, :].T)
        in_maps.append({
            "XT": xt, "W1L": W1, "W2L": W2, "W3L": W3,
            "WQL": WQ, "WKL": WK, "WVL": WV, "WOL": WO,
            "BQ": BQa, "BK": BKa, "BV": BVa, "BO": BOa,
            "IDENT": IDENTa,
            "HALF": np.array([[h]], np.int32), "NONCE": nonce,
        })
    nc = build_program()
    res = run_bass_kernel_spmd(nc, in_maps, core_ids=list(range(8)))
    out = np.empty((4, 2048, 2048), np.float32)
    for core in range(8):
        b, h = core // 2, core % 2
        out[b, h * 1024:(h + 1) * 1024, :] = np.asarray(res.results[core]["OUT"]).T
    return out
